# version 31
# baseline (speedup 1.0000x reference)
import numpy as np
import ml_dtypes
import concourse.bass as bass
import concourse.mybir as mybir
from concourse.bass import IndirectOffsetOnAxis
from concourse.bass_utils import run_bass_kernel_spmd
from contextlib import ExitStack

F32 = mybir.dt.float32
BF16 = mybir.dt.bfloat16
U32 = mybir.dt.uint32
AF = mybir.ActivationFunctionType
ALU = mybir.AluOpType
AX = mybir.AxisListType

T = 4096
D = 1024
NT = T // 128
H = 16
DH = 64
NG = 8
SB = 8
TC = 32
E = 32
HE = 512
HS = 1024
EPS = 1e-5
TMAX = 8
NP = E * 128 * TMAX

ENGS = ("v", "s", "t", "g", "y")
DEBUG = False
NOSCAT = False
EXPD = 4


def E_(name, *a, **k):
    def run(eng):
        return getattr(eng, name)(*a, **k)
    return run


class Em:

    NDMA = 12

    def __init__(self, nc, stack):
        self.nc = nc
        self.stack = stack
        self.phase = -1
        self.qs = []
        self.sems = {}
        self.cnt = {}
        self.dmakeys = set()
        for q in ("y", "g"):
            for k in range(self.NDMA):
                key = f"{q}dma{k}"
                self.sems[key] = stack.enter_context(nc.semaphore(key))
                self.cnt[key] = 0
                self.dmakeys.add(key)
        self.dma_rr = {"y": 0, "g": 0}
        self.new_phase()

    def new_phase(self):
        first = self.phase < 0
        self.phase += 1
        old_sems = None if first else dict(self.sems)
        old_cnt = None if first else dict(self.cnt)
        for e in ENGS:
            key = f"p{self.phase}_{e}"
            self.sems[e] = self.stack.enter_context(self.nc.semaphore(key))
            self.cnt[e] = 0
        self.q = {e: [] for e in ENGS}
        self.qs.append(self.q)
        self.waited = {e: {} for e in ENGS}
        self.lw = {}
        self.readers = {}
        if not first:
            for e in ENGS:
                for key, v in old_cnt.items():
                    if v > 0 and (key in self.dmakeys or key in ENGS):
                        sm = old_sems[key]
                        self.q[e].append((((sm, v),), None, 0, None))
                        self.waited[e][id(sm)] = max(
                            self.waited[e].get(id(sm), 0), v)

    def _need_wait(self, e, sm, val):
        key = id(sm)
        if self.waited[e].get(key, 0) >= val:
            return None
        self.waited[e][key] = val
        return (sm, val)

    def _wait_for(self, e, semkey, tick, waits):
        if semkey in self.dmakeys:
            val = self.cnt[semkey]
        else:
            val = tick
        nw = self._need_wait(e, self.sems[semkey], val)
        if nw:
            waits.append(nw)

    def op(self, e, fn, reads=(), writes=(), tick=True, dma=False):
        waits = []
        for r in reads:
            w = self.lw.get(r)
            if w is not None:
                wk, wt, weng = w
                if weng == e and (e == "t" or wt is None) and wk not in self.dmakeys:
                    continue
                assert wt is not None or wk in self.dmakeys, \
                    f"read of pending buffer {r} from {e}"
                self._wait_for(e, wk, wt, waits)
        for wname in writes:
            w = self.lw.get(wname)
            if w is not None:
                wk, wt, weng = w
                if not (weng == e and (e == "t" or wt is None)
                        and wk not in self.dmakeys):
                    assert wt is not None or wk in self.dmakeys, \
                        f"cross-engine WAW on pending {wname}"
                    self._wait_for(e, wk, wt, waits)
            for rk, (rt, reng) in self.readers.get(wname, {}).items():
                if reng == e and e == "t" and rk not in self.dmakeys:
                    continue
                assert rt is not None or rk in self.dmakeys, \
                    f"WAR on pending reader {rk} of {wname} (writer {e})"
                self._wait_for(e, rk, rt, waits)
        if dma:
            k = self.dma_rr[e]
            self.dma_rr[e] = (k + 1) % self.NDMA
            semkey = f"{e}dma{k}"
            inc = 16
        else:
            semkey = e
            inc = 1
        if tick or dma:
            self.cnt[semkey] += inc
            mytick = self.cnt[semkey]
        else:
            mytick = None
        self.q[e].append((tuple(waits), fn, inc if (tick or dma) else 0,
                          self.sems[semkey]))
        for wname in writes:
            self.lw[wname] = (semkey, mytick, e)
            self.readers[wname] = {}
        for r in reads:
            d = self.readers.setdefault(r, {})
            if mytick is not None or semkey not in d:
                d[semkey] = (mytick if mytick is not None
                             else (d.get(semkey) or (None,))[0], e)
        return mytick

    def run_queue(self, qi, e, eng):
        for waits, fn, inc, sm in self.qs[qi][e]:
            for (wsem, wval) in waits:
                eng.wait_ge(wsem, wval)
            if fn is not None:
                inst = fn(eng)
                if inc:
                    inst.then_inc(sm, inc)


def positional_encoding(seq_len, d_model):
    pos = np.arange(seq_len, dtype=np.float32)[:, None]
    div = np.exp(np.arange(0, d_model, 2, dtype=np.float32)
                 * (-np.log(10000.0) / d_model))
    pe = np.zeros((seq_len, d_model), dtype=np.float32)
    pe[:, 0::2] = np.sin(pos * div)
    pe[:, 1::2] = np.cos(pos * div)
    return pe


def build_program():
    nc = bass.Bass("TRN2", target_bir_lowering=False, debug=False)

    ap = {}
    def inp(name, shape, dt=F32):
        ap[name] = nc.dram_tensor(name, list(shape), dt, kind="ExternalInput").ap()
    inp("x", (T, D))
    inp("pe_tm", (T, D))
    inp("ident", (128, 128))
    inp("identbf", (128, 128), BF16)
    inp("iota32", (128, 32))
    inp("onescol2", (128, 2), BF16)
    inp("sel2", (2, 256))
    inp("u128", (128, 128), BF16)
    inp("allones", (128, 128), BF16)
    inp("tok4", (128, 4 * NT), U32)
    inp("dumpcol4", (128, 4))
    inp("Wq", (D, D)); inp("bq", (1, D)); inp("Wk", (D, D)); inp("bk", (1, D))
    inp("Wv", (D, D)); inp("bv", (1, D)); inp("Wo", (D, D)); inp("bo", (1, D))
    inp("gamma", (1, D)); inp("beta", (1, D))
    inp("Wg", (D, E))
    inp("W1", (E, D, HE)); inp("W2", (E, HE, D)); inp("W3", (E, D, HE))
    inp("Ws1", (D, HS)); inp("Ws2", (HS, D)); inp("Ws3", (D, HS))
    out = nc.dram_tensor("out", [T, D], F32, kind="ExternalOutput").ap()
    dbg = {}
    if DEBUG:
        for nm, shape, dt in (("d_s", [T, D], BF16), ("d_xn", [T, D], BF16),
                              ("d_gidx", [NP + 128, 4], U32),
                              ("d_pair", [NP + 128, D], BF16),
                              ("d_w", [128, NT * 4], F32),
                              ("d_pos", [128, NT * 4], U32),
                              ("d_hfm", [8, 128, T], BF16),
                              ("d_ot", [NT, 8, 128, 128], BF16)):
            dbg[nm] = nc.dram_tensor(nm, shape, dt,
                                     kind="ExternalOutput").ap()

    xn_tm = nc.dram_tensor("xn_tm", [T, D], BF16, kind="Internal").ap()
    h_tm = nc.dram_tensor("h_tm", [T, D], BF16, kind="Internal").ap()
    xn_fm = nc.dram_tensor("xn_fm", [8, 128, T], BF16, kind="Internal").ap()
    hfm_d = nc.dram_tensor("hfm_d", [8, 128, T], BF16, kind="Internal").ap()
    ot = nc.dram_tensor("ot", [NT, 8, 128, 128], BF16, kind="Internal").ap()
    s_tm = nc.dram_tensor("s_tm", [T, D], BF16, kind="Internal").ap()
    gidx = nc.dram_tensor("gidx", [NP + 128, 4], U32, kind="Internal").ap()
    pairout = nc.dram_tensor("pairout", [NP + 128, D], BF16,
                             kind="Internal").ap()

    with ExitStack() as g_stk:
        gen = g_stk.enter_context
        em = Em(nc, g_stk)

        rb_np = nc.alloc_register(mybir.EngineType.Pool, "bnd_np")
        rb_t = nc.alloc_register(mybir.EngineType.Pool, "bnd_t")
        em.op("g", E_("reg_mov", rb_np, NP + 127), tick=False)
        em.op("g", E_("reg_mov", rb_t, T - 1), tick=False)

        c_ones_bf = gen(nc.sbuf_tensor("c_ones_bf", [1, 512], BF16))
        c_ones_f = gen(nc.sbuf_tensor("c_ones_f", [1, 128], F32))
        w_all = gen(nc.sbuf_tensor("w_all", [128, NT * 4], F32))
        pos_all = gen(nc.sbuf_tensor("pos_all", [128, NT * 4], U32))

        def ld_const(nm, dst, q="y", rearr=None, tag=None):
            src = ap[nm] if rearr is None else ap[nm].rearrange(rearr, p=128)
            em.op(q, E_("dma_start", out=dst[:], in_=src),
                  writes=[tag or (nm + "_c")], dma=True)

        with ExitStack() as stk:
            en = stk.enter_context
            xb = [en(nc.sbuf_tensor(f"xb{i}", [128, D], F32)) for i in range(2)]
            pb = [en(nc.sbuf_tensor(f"pb{i}", [128, D], F32)) for i in range(2)]
            xnb = [en(nc.sbuf_tensor(f"xnb{i}", [128, D], F32)) for i in range(2)]
            hb = [en(nc.sbuf_tensor(f"hb{i}", [128, D], F32)) for i in range(2)]
            hbfb = [en(nc.sbuf_tensor(f"hbfb{i}", [128, D], BF16))
                    for i in range(2)]
            xtb = [en(nc.sbuf_tensor(f"xtb{i}", [128, D], F32)) for i in range(2)]
            xtbfb = [en(nc.sbuf_tensor(f"xtbfb{i}", [128, D], BF16))
                     for i in range(2)]
            htb = [en(nc.sbuf_tensor(f"htb{i}", [128, D], BF16))
                   for i in range(2)]
            sqs = en(nc.sbuf_tensor("sqs", [128, D], F32))
            stt = [en(nc.sbuf_tensor(f"stt{i}", [128, 12], F32))
                   for i in range(2)]
            scf = [en(nc.sbuf_tensor(f"scf{i}", [128, E], F32)) for i in range(2)]
            soft = en(nc.sbuf_tensor("soft", [128, E], F32))
            top8 = [en(nc.sbuf_tensor(f"top8_{i}", [128, 8], F32))
                    for i in range(2)]
            idx8 = en(nc.sbuf_tensor("idx8", [128, 8], U32))
            idxf = [en(nc.sbuf_tensor(f"idxf{i}", [128, 8], F32))
                    for i in range(2)]
            oh = en(nc.sbuf_tensor("oh", [128, E], F32))
            mbf = en(nc.sbuf_tensor("mbf", [128, E], BF16))
            rank_sb = [en(nc.sbuf_tensor(f"rank_sb{i}", [128, E], F32))
                       for i in range(2)]
            base_sb = en(nc.sbuf_tensor("base_sb", [128, E], F32))
            rtmp = en(nc.sbuf_tensor("rtmp", [128, 12], F32))
            w4e = en(nc.sbuf_tensor("w4e", [128, 4], F32))
            dump4_sb = en(nc.sbuf_tensor("dump4_sb", [128, 4], F32))
            zgid = en(nc.sbuf_tensor("zgid", [128, 64, 4], U32))
            zpo = en(nc.sbuf_tensor("zpo", [128, D], BF16))
            c_u128 = en(nc.sbuf_tensor("c_u128", [128, 128], BF16))
            c_allones = en(nc.sbuf_tensor("c_allones", [128, 128], BF16))
            c_tok4 = en(nc.sbuf_tensor("c_tok4", [128, 4 * NT], U32))
            ps32 = [en(nc.psum_tensor(f"ps32_{i}", [128, D], F32))
                    for i in range(2)]
            psbf = [en(nc.psum_tensor(f"psbf_{i}", [128, D], BF16))
                    for i in range(2)]
            psr = en(nc.psum_tensor("psr", [128, E], F32))
            psc = en(nc.psum_tensor("psc", [128, 2 * E], F32))
            c_ident = en(nc.sbuf_tensor("c_ident", [128, 128], F32))
            c_identbf = en(nc.sbuf_tensor("c_identbf", [128, 128], BF16))
            c_iota = en(nc.sbuf_tensor("c_iota", [128, 32], F32))
            wg_sb = en(nc.sbuf_tensor("wg_sb", [128, 8, E], F32))

            ld_const("ident", c_ident)
            ld_const("identbf", c_identbf)
            ld_const("iota32", c_iota)
            ld_const("u128", c_u128)
            ld_const("allones", c_allones)
            ld_const("tok4", c_tok4)
            ld_const("Wg", wg_sb, rearr="(c p) e -> p c e", tag="wg_sb")
            em.op("v", E_("memset", c_ones_bf[:], 1.0), writes=["ones_bf"])
            em.op("v", E_("memset", c_ones_f[:], 1.0), writes=["ones_f"])
            em.op("v", E_("memset", base_sb[:], 0.0), writes=["base_sb"])
            ld_const("dumpcol4", dump4_sb, tag="dump4_c")
            em.op("v", E_("memset", zgid[:], 0), writes=["zgid"])
            em.op("v", E_("memset", zpo[:], 0.0), writes=["zpo"])
            NZR = (NP + 128) // 128
            for z0 in range(0, NZR, 64):
                zn = min(64, NZR - z0)
                em.op("g", E_("dma_start",
                              out=gidx[z0 * 128:(z0 + zn) * 128, :].rearrange(
                                  "(a p) o -> p a o", p=128),
                              in_=zgid[:, 0:zn, :]),
                      reads=["zgid"], writes=[f"gidx_z{z0}"], dma=True)
            em.op("g", E_("dma_start", out=pairout[NP:NP + 128, :],
                          in_=zpo[:]),
                  reads=["zpo"], dma=True)

            def ph0_load(i):
                j = i % 2
                r0, r1 = i * 128, (i + 1) * 128
                em.op("y", E_("dma_start", out=xb[j][:], in_=ap["x"][r0:r1, :]),
                      writes=[f"x{j}"], dma=True)
                em.op("y", E_("dma_start", out=pb[j][:],
                              in_=ap["pe_tm"][r0:r1, :]),
                      writes=[f"pe{j}"], dma=True)

            def stage2(k):
                j = k % 2
                nst = f"st{j}"
                em.op("s", E_("activation", w4e[:], top8[j][:, 0:4], AF.Exp,
                              bias=stt[j][:, 8:9]),
                      reads=[f"top8_{j}", nst + "m"], writes=["w4e"])
                em.op("v", E_("tensor_single_scalar", w4e[:], w4e[:],
                              stt[j][:, 10:11], ALU.mult),
                      reads=["w4e", nst + "rd"], writes=["w4e"])
                for jj in range(4):
                    em.op("v", E_("tensor_single_scalar", oh[:], c_iota[:],
                                  idxf[j][:, jj:jj + 1], ALU.is_equal),
                          reads=["iota32_c", f"idxf{j}"], writes=["oh"])
                    em.op("v", E_("tensor_mul", oh[:], oh[:], rank_sb[j][:]),
                          reads=["oh", f"rank{j}"], writes=["oh"])
                    em.op("v", E_("tensor_reduce", rtmp[:, jj:jj + 1], oh[:],
                                  AX.X, ALU.add),
                          reads=["oh"], writes=["rtmp"])
                em.op("v", E_("tensor_scalar_mul", rtmp[:, 4:8],
                              idxf[j][:, 0:4], float(TMAX * 128)),
                      reads=[f"idxf{j}"], writes=["rtmp"])
                em.op("v", E_("tensor_add", rtmp[:, 4:8], rtmp[:, 4:8],
                              rtmp[:, 0:4]),
                      reads=["rtmp"], writes=["rtmp"])
                em.op("v", E_("tensor_single_scalar", rtmp[:, 8:12],
                              rtmp[:, 0:4], float(TMAX * 128), ALU.is_lt),
                      reads=["rtmp"], writes=["rtmp"])
                em.op("v", E_("tensor_mul", w_all[:, 4 * k:4 * k + 4],
                              w4e[:], rtmp[:, 8:12]),
                      reads=["w4e", "rtmp"], writes=["w_all"])
                em.op("v", E_("scalar_tensor_tensor", rtmp[:, 4:8],
                              rtmp[:, 4:8], dump4_sb[:, 0:1], rtmp[:, 8:12],
                              ALU.subtract, ALU.mult),
                      reads=["rtmp", "dump4_c"], writes=["rtmp"])
                em.op("v", E_("tensor_single_scalar", rtmp[:, 4:8],
                              rtmp[:, 4:8], dump4_sb[:, 0:1], ALU.add),
                      reads=["rtmp", "dump4_c"], writes=["rtmp"])
                em.op("v", E_("tensor_copy", pos_all[:, 4 * k:4 * k + 4],
                              rtmp[:, 4:8]),
                      reads=["rtmp"], writes=["pos_all"])
                if NOSCAT:
                    return
                for jj in range(4):
                    pj = 4 * k + jj
                    em.op("g", E_("indirect_dma_start", out=gidx,
                                  out_offset=IndirectOffsetOnAxis(
                                      pos_all[:, pj:pj + 1], axis=0),
                                  in_=c_tok4[:, 4 * k:4 * k + 4],
                                  in_offset=None, bounds_check=rb_np,
                                  oob_is_err=False),
                          reads=["pos_all", "tok4_c"]
                          + [f"gidx_z{z0}" for z0 in range(0, 257, 64)],
                          dma=True)

            ph0_load(0)
            for i in range(NT):
                j = i % 2
                r0, r1 = i * 128, (i + 1) * 128
                if i + 1 < NT:
                    ph0_load(i + 1)
                nx, np_, nxn, nh = f"x{j}", f"pe{j}", f"xn{j}", f"h{j}"
                nhbf, nst, nxt = f"hbf{j}", f"st{j}", f"xt{j}"
                nxtbf, nht = f"xtbf{j}", f"ht{j}"
                st = stt[j]
                em.op("v", E_("tensor_reduce", st[:, 0:1], xb[j][:], AX.X,
                              ALU.add), reads=[nx], writes=[nst])
                em.op("s", E_("activation", sqs[:], xb[j][:], AF.Square,
                              accum_out=st[:, 1:2]),
                      reads=[nx], writes=["sqs", nst + "q"])
                em.op("v", E_("tensor_scalar_mul", st[:, 2:3], st[:, 0:1],
                              1.0 / D), reads=[nst], writes=[nst])
                em.op("v", E_("tensor_mul", st[:, 3:4], st[:, 2:3],
                              st[:, 2:3]),
                      reads=[nst], writes=[nst])
                if i >= 1:
                    stage2(i - 1)
                em.op("v", E_("tensor_scalar", st[:, 4:5], st[:, 1:2],
                              1.0 / D, EPS, ALU.mult, ALU.add),
                      reads=[nst, nst + "q"], writes=[nst])
                em.op("v", E_("tensor_sub", st[:, 4:5], st[:, 4:5],
                              st[:, 3:4]),
                      reads=[nst], writes=[nst])
                em.op("s", E_("activation", st[:, 5:6], st[:, 4:5], AF.Sqrt),
                      reads=[nst], writes=[nst + "s"])
                em.op("v", E_("reciprocal", st[:, 6:7], st[:, 5:6]),
                      reads=[nst + "s"], writes=[nst + "r"])
                em.op("v", E_("tensor_scalar", xnb[j][:], xb[j][:],
                              st[:, 2:3], st[:, 6:7], ALU.subtract,
                              ALU.mult),
                      reads=[nx, nst, nst + "r"], writes=[nxn])
                em.op("v", E_("tensor_add", hb[j][:], xnb[j][:], pb[j][:]),
                      reads=[nxn, np_], writes=[nh])
                em.op("s", E_("activation", hbfb[j][:], hb[j][:], AF.Copy),
                      reads=[nh], writes=[nhbf])
                em.op("g", E_("dma_start", out=xn_tm[r0:r1, :], in_=xnb[j][:]),
                      reads=[nxn], dma=True)
                em.op("y", E_("dma_start", out=h_tm[r0:r1, :], in_=hbfb[j][:]),
                      reads=[nhbf], dma=True)
                for c in range(8):
                    em.op("t", E_("transpose", ps32[j][:, c * 128:(c + 1) * 128],
                                  xnb[j][:, c * 128:(c + 1) * 128], c_ident[:]),
                          reads=[nxn, "ident_c"], writes=[f"ps32_{j}"],
                          tick=(c == 7))
                    em.op("t", E_("transpose", psbf[j][:, c * 128:(c + 1) * 128],
                                  hbfb[j][:, c * 128:(c + 1) * 128],
                                  c_identbf[:]),
                          reads=[nhbf, "identbf_c"], writes=[f"psbf_{j}"],
                          tick=(c == 7))
                em.op("v", E_("tensor_copy", xtb[j][:], ps32[j][:]),
                      reads=[f"ps32_{j}"], writes=[nxt])
                em.op("v", E_("tensor_copy", xtbfb[j][:], ps32[j][:]),
                      reads=[f"ps32_{j}"], writes=[nxtbf])
                em.op("v", E_("tensor_copy", htb[j][:], psbf[j][:]),
                      reads=[f"psbf_{j}"], writes=[nht])
                em.op("g", E_("dma_start",
                              out=xn_fm[:, :, r0:r1].rearrange("c p t -> p c t"),
                              in_=xtbfb[j].ap().rearrange("p (c t) -> p c t",
                                                          c=8)),
                      reads=[nxtbf], dma=True)
                em.op("y", E_("dma_start",
                              out=hfm_d[:, :, r0:r1].rearrange("c p t -> p c t"),
                              in_=htb[j].ap().rearrange("p (c t) -> p c t",
                                                        c=8)),
                      reads=[nht], dma=True)
                for c in range(8):
                    em.op("t", E_("matmul", psr[:],
                                  xtb[j][:, c * 128:(c + 1) * 128],
                                  wg_sb[:, c, :], start=(c == 0), stop=(c == 7)),
                          reads=[nxt, "wg_sb"], writes=["psr"], tick=(c == 7))
                em.op("v", E_("tensor_copy", scf[j][:], psr[:]),
                      reads=["psr"], writes=[f"sc{j}"])
                em.op("v", E_("tensor_reduce", st[:, 8:9], scf[j][:], AX.X,
                              ALU.max, negate=True),
                      reads=[f"sc{j}"], writes=[nst + "m"])
                em.op("s", E_("activation", soft[:], scf[j][:], AF.Exp,
                              bias=st[:, 8:9], accum_out=st[:, 9:10]),
                      reads=[f"sc{j}", nst + "m"], writes=["soft", nst + "d"])
                em.op("v", E_("reciprocal", st[:, 10:11], st[:, 9:10]),
                      reads=[nst + "d"], writes=[nst + "rd"])
                em.op("v", E_("max", top8[j][:], scf[j][:]),
                      reads=[f"sc{j}"], writes=[f"top8_{j}"])
                em.op("v", E_("max_index", idx8[:], top8[j][:], scf[j][:]),
                      reads=[f"top8_{j}", f"sc{j}"], writes=["idx8"])
                em.op("v", E_("tensor_copy", idxf[j][:], idx8[:]),
                      reads=["idx8"], writes=[f"idxf{j}"])
                em.op("v", E_("tensor_single_scalar", oh[:], scf[j][:],
                              top8[j][:, 3:4], ALU.is_ge),
                      reads=[f"sc{j}", f"top8_{j}"], writes=["oh"])
                em.op("s", E_("activation", mbf[:], oh[:], AF.Copy),
                      reads=["oh"], writes=["mbf"])
                em.op("t", E_("matmul", psc[:, 0:E], c_u128[:], mbf[:],
                              start=True, stop=True),
                      reads=["u128_c", "mbf"], writes=["psc_r"])
                em.op("t", E_("matmul", psc[:, E:2 * E], c_allones[:], mbf[:],
                              start=True, stop=True),
                      reads=["allones_c", "mbf"], writes=["psc_t"])
                em.op("v", E_("tensor_add", rank_sb[j][:], psc[:, 0:E],
                              base_sb[:]),
                      reads=["psc_r", "base_sb"], writes=[f"rank{j}"])
                em.op("v", E_("tensor_add", base_sb[:], base_sb[:],
                              psc[:, E:2 * E]),
                      reads=["base_sb", "psc_t"], writes=["base_sb"])
            stage2(NT - 1)
            em.new_phase()

        with ExitStack() as stk:
            en = stk.enter_context
            hfm = en(nc.sbuf_tensor("hfm", [128, 8, T], BF16))
            qg = en(nc.sbuf_tensor("qg", [128, T], BF16))
            kg = en(nc.sbuf_tensor("kg", [128, T], BF16))
            vkm = en(nc.sbuf_tensor("vkm", [128, TC, 130], BF16))
            wqs = en(nc.sbuf_tensor("wqs", [128, 8, 128], BF16))
            wks = en(nc.sbuf_tensor("wks", [128, 8, 128], BF16))
            wvs = en(nc.sbuf_tensor("wvs", [128, 8, 128], BF16))
            qsq = [en(nc.sbuf_tensor(f"qsq{i}", [128, 512], BF16))
                   for i in range(2)]
            nst2 = en(nc.sbuf_tensor("nst2", [2, 8], F32))
            negm = en(nc.sbuf_tensor("negm", [128, 2], F32))
            qn_sb = en(nc.sbuf_tensor("qn_sb", [2, 512], F32))
            expst = [[en(nc.sbuf_tensor(f"expst{h}_{i}", [128, 512], BF16))
                      for i in range(EXPD)] for h in range(2)]
            l_bf = [en(nc.sbuf_tensor(f"l_bf{i}", [1, 512], BF16))
                    for i in range(2)]
            rb_sb = [en(nc.sbuf_tensor(f"rb_sb{i}", [64, 512], F32))
                     for i in range(2)]
            ost = [en(nc.sbuf_tensor(f"ost{i}", [64, 512], BF16))
                   for i in range(2)]
            c_oc2 = en(nc.sbuf_tensor("c_oc2", [128, 2], BF16))
            c_sel2 = en(nc.sbuf_tensor("c_sel2", [2, 256], F32))
            b_q = en(nc.sbuf_tensor("b_q", [1, D], BF16))
            b_k = en(nc.sbuf_tensor("b_k", [1, D], BF16))
            b_v = en(nc.sbuf_tensor("b_v", [1, D], BF16))
            w1s = en(nc.sbuf_tensor("w1s", [128, 8, HE], BF16))
            w3s = en(nc.sbuf_tensor("w3s", [128, 8, HE], BF16))
            w2s = en(nc.sbuf_tensor("w2s", [128, 4, D], BF16))
            gi_sb = [en(nc.sbuf_tensor(f"gi_sb{i}", [128, 4, 4], U32))
                     for i in range(2)]
            xg = [en(nc.sbuf_tensor(f"xg{i}", [128, 4, D], BF16))
                  for i in range(2)]
            xgt = [en(nc.sbuf_tensor(f"xgt{i}", [128, 8, 512], BF16))
                   for i in range(2)]
            asil = en(nc.sbuf_tensor("asil", [128, 4, 512], BF16))
            silt = en(nc.sbuf_tensor("silt", [128, 512], BF16))
            po_sb = [en(nc.sbuf_tensor(f"po_sb{i}", [128, D], BF16))
                     for i in range(2)]
            c_idbf2 = en(nc.sbuf_tensor("c_idbf2", [128, 128], BF16))
            bank = [en(nc.psum_tensor(f"bank{i}", [128, 512], F32))
                    for i in range(6)]
            psT = en(nc.psum_tensor("psT", [128, 1024], BF16))
            psX = en(nc.psum_tensor("psX", [128, 512], F32))

            def wload(g):
                for nm, dst, w in (("wqs", wqs, "Wq"), ("wks", wks, "Wk"),
                                   ("wvs", wvs, "Wv")):
                    em.op("g", E_("dma_start", out=dst[:],
                                  in_=ap[w][:, g * 128:(g + 1) * 128].rearrange(
                                      "(c p) n -> p c n", p=128)),
                          writes=[nm], dma=True)

            def a1_gen():
                ld_const("onescol2", c_oc2)
                ld_const("sel2", c_sel2)
                for nm, dst in (("bq", b_q), ("bk", b_k), ("bv", b_v)):
                    ld_const(nm, dst, q="g", tag=nm)
                for c in range(8):
                    em.op("y", E_("dma_start", out=hfm[:, c, :], in_=hfm_d[c]),
                          writes=["hfm"], dma=True)
                em.op("v", E_("memset", vkm[:], 0.0), writes=["vkm"])
                em.op("v", E_("memset", vkm[:, :, 64:65], 1.0), writes=["vkm1"])
                em.op("v", E_("memset", vkm[:, :, 129:130], 1.0),
                      writes=["vkm1"])
                wload(0)
                yield 0
                for g in range(NG):
                    gsl = slice(g * 128, (g + 1) * 128)
                    for (dst, wsb, wn, brow, bn, bks) in (
                            (qg, wqs, "wqs", b_q, "bq", (0, 2)),
                            (kg, wks, "wks", b_k, "bk", (1, 3))):
                        for sb in range(SB):
                            ssl = slice(sb * 512, (sb + 1) * 512)
                            bi = bks[sb % 2]
                            ps, pn = bank[bi], f"bk{bi}"
                            for c in range(8):
                                em.op("t", E_("matmul", ps[:], wsb[:, c, :],
                                              hfm[:, c, ssl], start=(c == 0),
                                              stop=False),
                                      reads=[wn, "hfm"], writes=[pn],
                                      tick=(c == 7))
                            em.op("t", E_("matmul", ps[:], brow[:, gsl],
                                          c_ones_bf[:], start=False,
                                          stop=True),
                                  reads=[bn, "ones_bf"], writes=[pn])
                            em.op("v", E_("tensor_copy", dst[:, ssl], ps[:]),
                                  reads=[pn], writes=[wn + "o"])
                            yield 2000
                    for i in range(NT):
                        tsl = slice(i * 128, (i + 1) * 128)
                        bi = (2, 5)[i % 2]
                        psv, pvn = bank[bi], f"bk{bi}"
                        for c in range(8):
                            em.op("t", E_("matmul", psv[:, 0:128],
                                          hfm[:, c, tsl],
                                          wvs[:, c, :], start=(c == 0),
                                          stop=False),
                                  reads=["wvs", "hfm"], writes=[pvn],
                                  tick=(c == 7))
                        em.op("t", E_("matmul", psv[:, 0:128],
                                      c_ones_bf[:, 0:128],
                                      b_v[:, gsl], start=False, stop=True),
                              reads=["bv", "ones_bf"], writes=[pvn])
                        em.op("v", E_("tensor_copy", vkm[:, i, 0:64],
                                      psv[:, 0:64]),
                              reads=[pvn], writes=["vkm"])
                        em.op("v", E_("tensor_copy", vkm[:, i, 65:129],
                                      psv[:, 64:128]),
                              reads=[pvn], writes=["vkm"])
                        yield 1100
                    if g + 1 < NG:
                        wload(g + 1)
                    for qk, (src, srcn) in enumerate(((qg, "wqso"),
                                                      (kg, "wkso"))):
                        col = slice(qk, qk + 1)
                        em.op("v", E_("memset", nst2[:, col], 0.0),
                              writes=["nst2"])

                        def mstep(sb, col=col):
                            bi = (3, 0)[sb % 2]
                            em.op("t", E_("matmul", bank[bi][0:2, :],
                                          c_oc2[:], qsq[sb % 2][:],
                                          start=True, stop=True),
                                  reads=["onescol2_c", f"qsq{sb % 2}"],
                                  writes=[f"bk{bi}"])
                            em.op("v", E_("tensor_reduce", qn_sb[:, 0:1],
                                          bank[bi][0:2, :], AX.X, ALU.max),
                                  reads=[f"bk{bi}"], writes=["qn_sb"])
                            em.op("v", E_("tensor_max", nst2[:, col],
                                          nst2[:, col], qn_sb[:, 0:1]),
                                  reads=["nst2", "qn_sb"], writes=["nst2"])

                        for sb in range(SB):
                            ssl = slice(sb * 512, (sb + 1) * 512)
                            em.op("s", E_("activation", qsq[sb % 2][:],
                                          src[:, ssl], AF.Square),
                                  reads=[srcn], writes=[f"qsq{sb % 2}"])
                            if sb >= 1:
                                mstep(sb - 1)
                            yield 300
                        mstep(SB - 1)
                        yield 300
                    em.op("v", E_("tensor_mul", nst2[:, 2:3], nst2[:, 0:1],
                                  nst2[:, 1:2]), reads=["nst2"],
                          writes=["nst2"])
                    em.op("s", E_("activation", nst2[:, 3:4], nst2[:, 2:3],
                                  AF.Sqrt), reads=["nst2"], writes=["nst2s"])
                    em.op("v", E_("tensor_scalar_mul", nst2[:, 4:5],
                                  nst2[:, 3:4], -0.125 * 1.02),
                          reads=["nst2s"], writes=["nst2m"])
                    for hh in range(2):
                        em.op("t", E_("matmul", bank[4][:, 0:1],
                                      c_sel2[:, hh * 128:(hh + 1) * 128],
                                      nst2[:, 4:5], start=True, stop=True),
                              reads=["sel2_c", "nst2m"], writes=["bk4"])
                        em.op("v", E_("tensor_copy", negm[:, hh:hh + 1],
                                      bank[4][:, 0:1]), reads=["bk4"],
                              writes=["negm"])
                    yield 400
                    for sb in range(SB):
                        s0 = sb * 512
                        qsl = slice(s0, s0 + 512)
                        for tcx in range(TC + 3):
                            if tcx >= 3:
                                tc = tcx - 3
                                ej = tc % EXPD
                                for hh in range(2):
                                    em.op("t", E_("matmul",
                                                  bank[4 + hh][0:65, :],
                                                  vkm[:, tc, 65 * hh:
                                                      65 * hh + 65],
                                                  expst[hh][ej][:],
                                                  start=(tc == 0),
                                                  stop=(tc == TC - 1)),
                                          reads=["vkm", "vkm1",
                                                 f"ex{hh}_{ej}"],
                                          writes=[f"bk{4 + hh}"], tick=True)
                                yield 700
                            if tcx < TC:
                                tc = tcx
                                tsl = slice(tc * 128, (tc + 1) * 128)
                                dj = (tc + sb) % 2
                                ej = tc % EXPD
                                em.op("t", E_("matmul", bank[dj][:],
                                              kg[0:64, tsl], qg[0:64, qsl],
                                              start=True, stop=True),
                                      reads=["wkso", "wqso"],
                                      writes=[f"bk{dj}"], tick=True)
                                em.op("t", E_("matmul", bank[2 + dj][:],
                                              kg[64:128, tsl],
                                              qg[64:128, qsl],
                                              start=True, stop=True),
                                      reads=["wkso", "wqso"],
                                      writes=[f"bk{2 + dj}"], tick=True)
                                for hh in range(2):
                                    em.op("s", E_("activation",
                                                  expst[hh][ej][:],
                                                  bank[2 * hh + dj][:],
                                                  AF.Exp,
                                                  bias=negm[:, hh:hh + 1],
                                                  scale=0.125),
                                          reads=[f"bk{2 * hh + dj}", "negm"],
                                          writes=[f"ex{hh}_{ej}"])
                                yield 600
                        for hh in range(2):
                            em.op("v", E_("tensor_copy", l_bf[hh][:],
                                          bank[4 + hh][64:65, :]),
                                  reads=[f"bk{4 + hh}"],
                                  writes=[f"l_bf{hh}"])
                        yield 300
                        for hh in range(2):
                            em.op("t", E_("matmul", bank[4 + hh][64:128, :],
                                          c_ones_bf[:, 0:64], l_bf[hh][:],
                                          start=True, stop=True),
                                  reads=["ones_bf", f"l_bf{hh}"],
                                  writes=[f"bk{4 + hh}"], tick=True)
                            em.op("v", E_("reciprocal", rb_sb[hh][:],
                                          bank[4 + hh][64:128, :]),
                                  reads=[f"bk{4 + hh}"],
                                  writes=[f"rb_sb{hh}"])
                        yield 300
                        for hh in range(2):
                            gh = 2 * g + hh
                            em.op("v", E_("tensor_mul", ost[hh][:],
                                          bank[4 + hh][0:64, :],
                                          rb_sb[hh][:]),
                                  reads=[f"bk{4 + hh}", f"rb_sb{hh}"],
                                  writes=[f"ost{hh}"])
                            em.op("y", E_("dma_start",
                                          out=ot[sb * 4:(sb + 1) * 4,
                                                 gh // 2,
                                                 64 * (gh % 2):
                                                 64 * (gh % 2) + 64,
                                                 :].rearrange(
                                              "a r x -> r a x"),
                                          in_=ost[hh].ap().rearrange(
                                              "r (a x) -> r a x", a=4)),
                                  reads=[f"ost{hh}"], dma=True)
                            yield 500

            NST = E * 2

            def bload(e):
                for nm, dst, w in (("w1s", w1s, "W1"), ("w3s", w3s, "W3"),
                                   ("w2s", w2s, "W2")):
                    em.op("g", E_("dma_start", out=dst[:],
                                  in_=ap[w][e].rearrange("(c p) h -> p c h",
                                                         p=128)),
                          writes=[nm], dma=True)

            def gload(st_):
                sj = st_ % 2
                base = st_ * 512
                em.op("y", E_("dma_start", out=gi_sb[sj][:],
                              in_=gidx[base:base + 512, :].rearrange(
                                  "(a p) o -> p a o", p=128)),
                      writes=[f"gi{sj}"], dma=True)
                for sub in range(4):
                    em.op("g", E_("indirect_dma_start", out=xg[sj][:, sub, :],
                                  out_offset=None, in_=xn_tm,
                                  in_offset=IndirectOffsetOnAxis(
                                      gi_sb[sj][:, sub, 0:1], axis=0),
                                  bounds_check=rb_t, oob_is_err=False),
                          reads=[f"gi{sj}"], writes=[f"xg{sj}"], dma=True)

            def trs(st_, c):
                sj = st_ % 2
                for sub in range(4):
                    em.op("t", E_("transpose",
                                  psT[:, sub * 128:(sub + 1) * 128],
                                  xg[sj][:, sub,
                                         c * 128:(c + 1) * 128],
                                  c_idbf2[:]),
                          reads=[f"xg{sj}", "idbf2_c"],
                          writes=["psT"], tick=(sub == 3))
                em.op("v", E_("tensor_copy", xgt[sj][:, c, :],
                              psT[:, 0:512]),
                      reads=["psT"], writes=[f"xgt{sj}"])

            def b_gen():
                ld_const("identbf", c_idbf2, tag="idbf2_c")
                bload(0)
                gload(0)
                gload(1)
                yield 0
                for c in range(8):
                    trs(0, c)
                    yield 700
                for st_ in range(NST):
                    e = st_ // 2
                    sj = st_ % 2
                    base = st_ * 512
                    if st_ + 2 < NST:
                        gload(st_ + 2)
                    for hc in range(4):
                        hsl = slice(hc * 128, (hc + 1) * 128)
                        for c in range(8):
                            em.op("t", E_("matmul", psX[:], w1s[:, c, hsl],
                                          xgt[sj][:, c, :], start=(c == 0),
                                          stop=(c == 7)),
                                  reads=["w1s", f"xgt{sj}"], writes=["psX"],
                                  tick=(c == 7))
                        em.op("s", E_("activation", silt[:], psX[:], AF.Silu),
                              reads=["psX"], writes=["silt"])
                        yield 2100
                        if st_ + 1 < NST:
                            trs(st_ + 1, 2 * hc)
                            yield 700
                        for c in range(8):
                            em.op("t", E_("matmul", psX[:], w3s[:, c, hsl],
                                          xgt[sj][:, c, :], start=(c == 0),
                                          stop=(c == 7)),
                                  reads=["w3s", f"xgt{sj}"], writes=["psX"],
                                  tick=(c == 7))
                        em.op("v", E_("tensor_mul", asil[:, hc, :], silt[:],
                                      psX[:]),
                              reads=["silt", "psX"], writes=["asil"])
                        yield 2100
                        if st_ + 1 < NST:
                            trs(st_ + 1, 2 * hc + 1)
                            yield 700
                    for tt in range(4):
                        tsl = slice(tt * 128, (tt + 1) * 128)
                        oj = tt % 2
                        for half in range(2):
                            dsl = slice(half * 512, (half + 1) * 512)
                            for hc in range(4):
                                em.op("t", E_("matmul", psX[:],
                                              asil[:, hc, tsl],
                                              w2s[:, hc, dsl],
                                              start=(hc == 0), stop=(hc == 3)),
                                      reads=["asil", "w2s"],
                                      writes=["psX"], tick=(hc == 3))
                            em.op("v", E_("tensor_copy", po_sb[oj][:, dsl],
                                          psX[:]),
                                  reads=["psX"], writes=[f"po{oj}"])
                            yield 1100
                        em.op("y", E_("dma_start",
                                      out=pairout[base + tt * 128:
                                                  base + (tt + 1) * 128, :],
                                      in_=po_sb[oj][:]),
                              reads=[f"po{oj}"], dma=True)
                    if st_ % 2 == 1 and e + 1 < E:
                        bload(e + 1)

            ga, gb = a1_gen(), b_gen()
            TA, TB = 3.51e6, 2.0e6
            pa, pb = 0.0, 0.0
            da = db = False
            pb += next(gb)
            while not (da and db):
                if db or (not da and pa * TB <= pb * TA):
                    try:
                        pa += next(ga)
                    except StopIteration:
                        da = True
                else:
                    try:
                        pb += next(gb)
                    except StopIteration:
                        db = True
            em.new_phase()

        with ExitStack() as stkO:
            with ExitStack() as stk2:
                en2 = stk2.enter_context
                wo_sb = en2(nc.sbuf_tensor("wo_sb", [128, 8, D], BF16))
                otile = [en2(nc.sbuf_tensor(f"otile{i}", [128, 8, 128], BF16))
                         for i in range(2)]
                htb2 = [en2(nc.sbuf_tensor(f"htb2{i}", [128, D], BF16))
                        for i in range(2)]
                yb = [en2(nc.sbuf_tensor(f"yb{i}", [128, D], F32))
                      for i in range(2)]
                sb_t = [en2(nc.sbuf_tensor(f"sb_t{i}", [128, D], F32))
                        for i in range(2)]
                sqs2 = en2(nc.sbuf_tensor("sqs2", [128, D], F32))
                st2 = [en2(nc.sbuf_tensor(f"st2_{i}", [128, 8], F32))
                       for i in range(2)]
                b_o = en2(nc.sbuf_tensor("b_o", [1, D], BF16))
                gam_r = en2(nc.sbuf_tensor("gam_r", [1, D], F32))
                bet_r = en2(nc.sbuf_tensor("bet_r", [1, D], F32))
                gam_b = en2(nc.sbuf_tensor("gam_b", [128, D], F32))
                bet_b = en2(nc.sbuf_tensor("bet_b", [128, D], F32))
                xnsb = [en2(nc.sbuf_tensor(f"xnsb{i}", [128, 8, 512], BF16))
                        for i in range(2)]
                pg = [en2(nc.sbuf_tensor(f"pg{i}", [128, 4, D], BF16))
                      for i in range(2)]
                ymt = [en2(nc.sbuf_tensor(f"ymt{i}", [128, D], F32))
                       for i in range(2)]
                ws1 = en2(nc.sbuf_tensor("ws1", [128, 8, HS], BF16))
                ws3_ = en2(nc.sbuf_tensor("ws3_", [128, 8, HS], BF16))
                ws2_ = en2(nc.sbuf_tensor("ws2_", [128, 8, D], BF16))
                wact = [en2(nc.sbuf_tensor(f"wact{i}", [128, 8, 512], BF16))
                        for i in range(2)]
                silt2 = en2(nc.sbuf_tensor("silt2", [128, 512], BF16))
                gbuf = en2(nc.sbuf_tensor("gbuf", [128, 512], F32))
                xb2 = [en2(nc.sbuf_tensor(f"xb2_{i}", [128, D], F32))
                       for i in range(2)]
                ob = [en2(nc.sbuf_tensor(f"ob{i}", [128, D], F32))
                      for i in range(2)]
                psw = en2(nc.psum_tensor("psw", [128, 1024], F32))
                psu2 = [en2(nc.psum_tensor(f"psu2_{i}", [128, 512], F32))
                        for i in range(2)]
                psv2 = [en2(nc.psum_tensor(f"psv2_{i}", [128, 512], F32))
                        for i in range(2)]
                psz = [en2(nc.psum_tensor(f"psz{i}", [128, 512], F32))
                       for i in range(2)]

                em.op("g", E_("dma_start", out=wo_sb[:],
                              in_=ap["Wo"].rearrange("(c p) n -> p c n",
                                                     p=128)),
                      writes=["wo_sb"], dma=True)
                ld_const("bo", b_o, q="g", tag="bo")
                ld_const("gamma", gam_r)
                ld_const("beta", bet_r)
                for nm, dst, w in (("ws1", ws1, "Ws1"), ("ws3_", ws3_, "Ws3"),
                                   ("ws2_", ws2_, "Ws2")):
                    em.op("g", E_("dma_start", out=dst[:],
                                  in_=ap[w].rearrange("(c p) h -> p c h",
                                                      p=128)),
                          writes=[nm], dma=True)
                for src, srcn, dst, dstn in ((gam_r, "gamma_c", gam_b,
                                              "gam_b"),
                                             (bet_r, "beta_c", bet_b,
                                              "bet_b")):
                    for half in range(2):
                        hsl = slice(half * 512, (half + 1) * 512)
                        em.op("t", E_("matmul", psz[0][:], c_ones_f[:],
                                      src[:, hsl], start=True, stop=True),
                              reads=["ones_f", srcn], writes=["psz0"])
                        em.op("v", E_("tensor_copy", dst[:, hsl], psz[0][:]),
                              reads=["psz0"], writes=[dstn])

                def a2_load(i):
                    j = i % 2
                    r0, r1 = i * 128, (i + 1) * 128
                    em.op("y", E_("dma_start", out=otile[j][:],
                                  in_=ot[i].rearrange("c p t -> p c t")),
                          writes=[f"otile{j}"], dma=True)
                    em.op("y", E_("dma_start", out=htb2[j][:],
                                  in_=h_tm[r0:r1, :]),
                          writes=[f"ht2{j}"], dma=True)

                def c_load(ti):
                    j = ti % 2
                    r0, r1 = ti * 128, (ti + 1) * 128
                    em.op("y", E_("dma_start", out=xb2[j][:],
                                  in_=ap["x"][r0:r1, :]),
                          writes=[f"x2_{j}"], dma=True)
                    for jj in range(4):
                        em.op("g", E_("indirect_dma_start",
                                      out=pg[j][:, jj, :], out_offset=None,
                                      in_=pairout,
                                      in_offset=IndirectOffsetOnAxis(
                                          pos_all[:, ti * 4 + jj:
                                                  ti * 4 + jj + 1], axis=0),
                                      bounds_check=rb_np,
                                      oob_is_err=False),
                              reads=["pos_all"], writes=[f"pg{j}"], dma=True)

                def xn_load(sb):
                    j2 = sb % 2
                    ssl = slice(sb * 512, (sb + 1) * 512)
                    em.op("y", E_("dma_start", out=xnsb[j2][:],
                                  in_=xn_fm[:, :, ssl].rearrange(
                                      "c p t -> p c t")),
                          writes=[f"xnsb{j2}"], dma=True)

                def chid_unit(sb2, hc):
                    j2 = sb2 % 2
                    hj = hc % 2
                    hsl = slice(hc * 128, (hc + 1) * 128)
                    for c in range(8):
                        em.op("t", E_("matmul", psu2[hj][:],
                                      ws1[:, c, hsl],
                                      xnsb[j2][:, c, :], start=(c == 0),
                                      stop=(c == 7)),
                              reads=["ws1", f"xnsb{j2}"],
                              writes=[f"psu2_{hj}"], tick=(c == 7))
                    em.op("s", E_("activation", silt2[:], psu2[hj][:],
                                  AF.Silu), reads=[f"psu2_{hj}"],
                          writes=["silt2"])
                    for c in range(8):
                        em.op("t", E_("matmul", psv2[hj][:],
                                      ws3_[:, c, hsl],
                                      xnsb[j2][:, c, :], start=(c == 0),
                                      stop=(c == 7)),
                              reads=["ws3_", f"xnsb{j2}"],
                              writes=[f"psv2_{hj}"], tick=(c == 7))
                    em.op("v", E_("tensor_mul", wact[j2][:, hc, :], silt2[:],
                                  psv2[hj][:]),
                          reads=["silt2", f"psv2_{hj}"],
                          writes=[f"wact{j2}"])

                a2_load(0)
                c_load(0)
                xn_load(0)
                xn_load(1)
                for hc in range(8):
                    chid_unit(0, hc)
                for sb in range(SB):
                    j2 = sb % 2
                    if sb + 2 < SB:
                        xn_load(sb + 2)
                    for tt in range(4):
                        ti = sb * 4 + tt
                        j = ti % 2
                        r0, r1 = ti * 128, (ti + 1) * 128
                        tsl = slice(tt * 128, (tt + 1) * 128)
                        if ti + 1 < NT:
                            a2_load(ti + 1)
                            c_load(ti + 1)
                        for half in range(2):
                            dsl = slice(half * 512, (half + 1) * 512)
                            for c in range(8):
                                em.op("t", E_("matmul", psw[:, dsl],
                                              otile[j][:, c, :],
                                              wo_sb[:, c, dsl],
                                              start=(c == 0), stop=False),
                                      reads=[f"otile{j}", "wo_sb"],
                                      writes=["psw"], tick=(c == 7))
                            em.op("t", E_("matmul", psw[:, dsl],
                                          c_ones_bf[:, 0:128],
                                          b_o[:, dsl], start=False,
                                          stop=True),
                                  reads=["bo", "ones_bf"], writes=["psw"])
                        em.op("v", E_("tensor_add", yb[j][:], psw[:],
                                      htb2[j][:]),
                              reads=["psw", f"ht2{j}"], writes=[f"y{j}"])
                        st = st2[j]
                        em.op("v", E_("tensor_reduce", st[:, 0:1], yb[j][:],
                                      AX.X, ALU.add),
                              reads=[f"y{j}"], writes=[f"st2{j}"])
                        em.op("s", E_("activation", sqs2[:], yb[j][:],
                                      AF.Square, accum_out=st[:, 1:2]),
                              reads=[f"y{j}"], writes=["sqs2", f"st2{j}q"])
                        em.op("v", E_("tensor_scalar_mul", st[:, 2:3],
                                      st[:, 0:1], 1.0 / D),
                              reads=[f"st2{j}"], writes=[f"st2{j}"])
                        em.op("v", E_("tensor_mul", st[:, 3:4], st[:, 2:3],
                                      st[:, 2:3]),
                              reads=[f"st2{j}"], writes=[f"st2{j}"])
                        em.op("v", E_("tensor_scalar", st[:, 4:5], st[:, 1:2],
                                      1.0 / D, EPS, ALU.mult, ALU.add),
                              reads=[f"st2{j}", f"st2{j}q"],
                              writes=[f"st2{j}"])
                        em.op("v", E_("tensor_sub", st[:, 4:5], st[:, 4:5],
                                      st[:, 3:4]),
                              reads=[f"st2{j}"], writes=[f"st2{j}"])
                        em.op("s", E_("activation", st[:, 5:6], st[:, 4:5],
                                      AF.Sqrt),
                              reads=[f"st2{j}"], writes=[f"st2{j}s"])
                        em.op("v", E_("reciprocal", st[:, 6:7], st[:, 5:6]),
                              reads=[f"st2{j}s"], writes=[f"st2{j}r"])
                        em.op("v", E_("tensor_scalar", yb[j][:], yb[j][:],
                                      st[:, 2:3], st[:, 6:7], ALU.subtract,
                                      ALU.mult),
                              reads=[f"y{j}", f"st2{j}", f"st2{j}r"],
                              writes=[f"y{j}"])
                        em.op("v", E_("tensor_mul", yb[j][:], yb[j][:],
                                      gam_b[:]),
                              reads=[f"y{j}", "gam_b"], writes=[f"y{j}"])
                        em.op("v", E_("tensor_add", sb_t[j][:], yb[j][:],
                                      bet_b[:]),
                              reads=[f"y{j}", "bet_b"], writes=[f"s{j}"])
                        if sb + 1 < SB:
                            chid_unit(sb + 1, 2 * tt)
                        for half in range(2):
                            dsl = slice(half * 512, (half + 1) * 512)
                            em.op("v", E_("tensor_single_scalar",
                                          psz[half][:],
                                          pg[j][:, 0, dsl],
                                          w_all[:, ti * 4:ti * 4 + 1],
                                          ALU.mult),
                                  reads=[f"pg{j}", "w_all"],
                                  writes=[f"psz{half}"])
                            for jj in range(1, 4):
                                em.op("v", E_("scalar_tensor_tensor",
                                              psz[half][:],
                                              pg[j][:, jj, dsl],
                                              w_all[:, ti * 4 + jj:
                                                    ti * 4 + jj + 1],
                                              psz[half][:], ALU.mult,
                                              ALU.add),
                                      reads=[f"pg{j}", "w_all",
                                             f"psz{half}"],
                                      writes=[f"psz{half}"])
                            for hc in range(8):
                                em.op("t", E_("matmul", psz[half][:],
                                              wact[j2][:, hc, tsl],
                                              ws2_[:, hc, dsl],
                                              start=False,
                                              stop=(hc == 7)),
                                      reads=[f"wact{j2}", "ws2_"],
                                      writes=[f"psz{half}"], tick=(hc == 7))
                            em.op("s", E_("activation", gbuf[:],
                                          psz[half][:],
                                          AF.Silu), reads=[f"psz{half}"],
                                  writes=["gbuf"])
                            em.op("v", E_("tensor_mul", gbuf[:], gbuf[:],
                                          sb_t[j][:, dsl]),
                                  reads=["gbuf", f"s{j}"], writes=["gbuf"])
                            em.op("v", E_("tensor_add", ob[j][:, dsl],
                                          xb2[j][:, dsl], gbuf[:]),
                                  reads=["gbuf", f"x2_{j}"],
                                  writes=[f"ob{j}"])
                        em.op("y", E_("dma_start", out=out[r0:r1, :],
                                      in_=ob[j][:]),
                              reads=[f"ob{j}"], dma=True)
                        if sb + 1 < SB:
                            chid_unit(sb + 1, 2 * tt + 1)
                if DEBUG:
                    em.op("y", E_("dma_start", out=dbg["d_s"], in_=s_tm),
                          dma=True)
                    em.op("y", E_("dma_start", out=dbg["d_xn"], in_=xn_tm),
                          dma=True)
                    em.op("y", E_("dma_start", out=dbg["d_gidx"], in_=gidx),
                          dma=True)
                    for z0 in range(0, NP + 128, 8192):
                        zn = min(8192, NP + 128 - z0)
                        em.op("y", E_("dma_start",
                                      out=dbg["d_pair"][z0:z0 + zn, :],
                                      in_=pairout[z0:z0 + zn, :]), dma=True)
                    em.op("y", E_("dma_start", out=dbg["d_w"], in_=w_all[:]),
                          reads=["w_all"], dma=True)
                    em.op("y", E_("dma_start", out=dbg["d_pos"],
                                  in_=pos_all[:]),
                          reads=["pos_all"], dma=True)
                    em.op("y", E_("dma_start", out=dbg["d_hfm"],
                                  in_=hfm_d), dma=True)
                    em.op("y", E_("dma_start", out=dbg["d_ot"], in_=ot),
                          dma=True)
                for key in sorted(em.dmakeys):
                    if em.cnt[key] > 0:
                        em.op("s", E_("wait_ge", em.sems[key], em.cnt[key]),
                              tick=False)

                for qi in range(len(em.qs)):
                    with nc.Block(f"blk{qi}") as blk:
                        blk.vector(lambda v, qi=qi: em.run_queue(qi, "v", v))
                        blk.scalar(lambda s, qi=qi: em.run_queue(qi, "s", s))
                        blk.tensor(lambda t, qi=qi: em.run_queue(qi, "t", t))
                        blk.gpsimd(lambda g, qi=qi: em.run_queue(qi, "g", g))
                        blk.sync(lambda y, qi=qi: em.run_queue(qi, "y", y))

    return nc


def kernel(**inputs):
    x_full = np.ascontiguousarray(np.asarray(inputs["L0_attn_x"], np.float32))
    B = x_full.shape[0]
    pe = positional_encoding(T, D)
    ident = np.eye(128, dtype=np.float32)
    identbf = np.eye(128).astype(ml_dtypes.bfloat16)
    iota32 = np.broadcast_to(np.arange(E, dtype=np.float32), (128, E)).copy()
    onescol2 = np.zeros((128, 2), ml_dtypes.bfloat16)
    onescol2[0:64, 0] = 1
    onescol2[64:128, 1] = 1
    sel2 = np.zeros((2, 256), np.float32)
    sel2[0, 0:128] = 1
    sel2[1, 128:256] = 1
    u128 = np.triu(np.ones((128, 128)), 1).astype(ml_dtypes.bfloat16)
    allones = np.ones((128, 128), ml_dtypes.bfloat16)
    tok4 = np.repeat((np.arange(NT, dtype=np.uint32)[None, :] * 128
                      + np.arange(128, dtype=np.uint32)[:, None]), 4, axis=1)
    tok4 = np.ascontiguousarray(tok4.astype(np.uint32))
    dumpcol4 = np.broadcast_to(
        (float(NP) + np.arange(128, dtype=np.float32)).reshape(128, 1),
        (128, 4)).copy()

    f32 = lambda k: np.ascontiguousarray(np.asarray(inputs[k], np.float32))
    row = lambda k: f32(k).reshape(1, -1)
    base = {
        "pe_tm": pe, "ident": ident, "identbf": identbf, "iota32": iota32,
        "onescol2": onescol2, "sel2": sel2, "u128": u128,
        "allones": allones, "tok4": tok4, "dumpcol4": dumpcol4,
        "Wq": f32("L0_attn_Wq"), "bq": row("L0_attn_bq"),
        "Wk": f32("L0_attn_Wk"), "bk": row("L0_attn_bk"),
        "Wv": f32("L0_attn_Wv"), "bv": row("L0_attn_bv"),
        "Wo": f32("L0_attn_Wo"), "bo": row("L0_attn_bo"),
        "gamma": row("L0_attn_gamma"), "beta": row("L0_attn_beta"),
        "Wg": f32("L0_ffn_Wg"),
        "W1": f32("L0_ffn_W1"), "W2": f32("L0_ffn_W2"), "W3": f32("L0_ffn_W3"),
        "Ws1": f32("L0_ffn_Ws1"), "Ws2": f32("L0_ffn_Ws2"),
        "Ws3": f32("L0_ffn_Ws3"),
    }
    nc = build_program()
    in_maps = [dict(base, x=np.ascontiguousarray(x_full[b])) for b in range(B)]
    res = run_bass_kernel_spmd(nc, in_maps, core_ids=list(range(B)))
    return np.stack([np.asarray(r["out"], np.float32) for r in res.results], 0)


# version 32
# speedup vs baseline: 1.0552x; 1.0552x over previous
import numpy as np
import ml_dtypes
import concourse.bass as bass
import concourse.mybir as mybir
from concourse.bass import IndirectOffsetOnAxis
from concourse.bass_utils import run_bass_kernel_spmd
from contextlib import ExitStack

F32 = mybir.dt.float32
BF16 = mybir.dt.bfloat16
U32 = mybir.dt.uint32
AF = mybir.ActivationFunctionType
ALU = mybir.AluOpType
AX = mybir.AxisListType

T = 4096
D = 1024
NT = T // 128
H = 16
DH = 64
NG = 8
SB = 8
TC = 32
E = 32
HE = 512
HS = 1024
EPS = 1e-5
TMAX = 8
NP = E * 128 * TMAX

ENGS = ("v", "s", "t", "g", "y")
DEBUG = False
NOSCAT = False
EXPD = 4


def E_(name, *a, **k):
    def run(eng):
        return getattr(eng, name)(*a, **k)
    return run


class Em:

    NDMA = 12

    def __init__(self, nc, stack):
        self.nc = nc
        self.stack = stack
        self.phase = -1
        self.qs = []
        self.sems = {}
        self.cnt = {}
        self.dmakeys = set()
        for q in ("y", "g"):
            for k in range(self.NDMA):
                key = f"{q}dma{k}"
                self.sems[key] = stack.enter_context(nc.semaphore(key))
                self.cnt[key] = 0
                self.dmakeys.add(key)
        self.dma_rr = {"y": 0, "g": 0}
        self.new_phase()

    def new_phase(self):
        first = self.phase < 0
        self.phase += 1
        old_sems = None if first else dict(self.sems)
        old_cnt = None if first else dict(self.cnt)
        for e in ENGS:
            key = f"p{self.phase}_{e}"
            self.sems[e] = self.stack.enter_context(self.nc.semaphore(key))
            self.cnt[e] = 0
        self.q = {e: [] for e in ENGS}
        self.qs.append(self.q)
        self.waited = {e: {} for e in ENGS}
        self.lw = {}
        self.readers = {}
        if not first:
            for e in ENGS:
                for key, v in old_cnt.items():
                    if v > 0 and (key in self.dmakeys or key in ENGS):
                        sm = old_sems[key]
                        self.q[e].append((((sm, v),), None, 0, None))
                        self.waited[e][id(sm)] = max(
                            self.waited[e].get(id(sm), 0), v)

    def _need_wait(self, e, sm, val):
        key = id(sm)
        if self.waited[e].get(key, 0) >= val:
            return None
        self.waited[e][key] = val
        return (sm, val)

    def _wait_for(self, e, semkey, tick, waits):
        if semkey in self.dmakeys:
            val = self.cnt[semkey]
        else:
            val = tick
        nw = self._need_wait(e, self.sems[semkey], val)
        if nw:
            waits.append(nw)

    def op(self, e, fn, reads=(), writes=(), tick=True, dma=False):
        waits = []
        for r in reads:
            w = self.lw.get(r)
            if w is not None:
                wk, wt, weng = w
                if weng == e and (e == "t" or wt is None) and wk not in self.dmakeys:
                    continue
                assert wt is not None or wk in self.dmakeys, \
                    f"read of pending buffer {r} from {e}"
                self._wait_for(e, wk, wt, waits)
        for wname in writes:
            w = self.lw.get(wname)
            if w is not None:
                wk, wt, weng = w
                if not (weng == e and (e == "t" or wt is None)
                        and wk not in self.dmakeys):
                    assert wt is not None or wk in self.dmakeys, \
                        f"cross-engine WAW on pending {wname}"
                    self._wait_for(e, wk, wt, waits)
            for rk, (rt, reng) in self.readers.get(wname, {}).items():
                if reng == e and e == "t" and rk not in self.dmakeys:
                    continue
                assert rt is not None or rk in self.dmakeys, \
                    f"WAR on pending reader {rk} of {wname} (writer {e})"
                self._wait_for(e, rk, rt, waits)
        if dma:
            k = self.dma_rr[e]
            self.dma_rr[e] = (k + 1) % self.NDMA
            semkey = f"{e}dma{k}"
            inc = 16
        else:
            semkey = e
            inc = 1
        if tick or dma:
            self.cnt[semkey] += inc
            mytick = self.cnt[semkey]
        else:
            mytick = None
        self.q[e].append((tuple(waits), fn, inc if (tick or dma) else 0,
                          self.sems[semkey]))
        for wname in writes:
            self.lw[wname] = (semkey, mytick, e)
            self.readers[wname] = {}
        for r in reads:
            d = self.readers.setdefault(r, {})
            if mytick is not None or semkey not in d:
                d[semkey] = (mytick if mytick is not None
                             else (d.get(semkey) or (None,))[0], e)
        return mytick

    def run_queue(self, qi, e, eng):
        for waits, fn, inc, sm in self.qs[qi][e]:
            for (wsem, wval) in waits:
                eng.wait_ge(wsem, wval)
            if fn is not None:
                inst = fn(eng)
                if inc:
                    inst.then_inc(sm, inc)


def positional_encoding(seq_len, d_model):
    pos = np.arange(seq_len, dtype=np.float32)[:, None]
    div = np.exp(np.arange(0, d_model, 2, dtype=np.float32)
                 * (-np.log(10000.0) / d_model))
    pe = np.zeros((seq_len, d_model), dtype=np.float32)
    pe[:, 0::2] = np.sin(pos * div)
    pe[:, 1::2] = np.cos(pos * div)
    return pe


def build_program():
    nc = bass.Bass("TRN2", target_bir_lowering=False, debug=False)

    ap = {}
    def inp(name, shape, dt=F32):
        ap[name] = nc.dram_tensor(name, list(shape), dt, kind="ExternalInput").ap()
    inp("x", (T, D))
    inp("pe_tm", (T, D))
    inp("ident", (128, 128))
    inp("identbf", (128, 128), BF16)
    inp("iota32", (128, 32))
    inp("onescol2", (128, 2), BF16)
    inp("sel2", (2, 256))
    inp("u128", (128, 128), BF16)
    inp("allones", (128, 128), BF16)
    inp("tok4", (128, 4 * NT), U32)
    inp("dumpcol4", (128, 4))
    inp("Wq", (D, D)); inp("bq", (1, D)); inp("Wk", (D, D)); inp("bk", (1, D))
    inp("Wv", (D, D)); inp("bv", (1, D)); inp("Wo", (D, D)); inp("bo", (1, D))
    inp("gamma", (1, D)); inp("beta", (1, D))
    inp("Wg", (D, E))
    inp("W1", (E, D, HE)); inp("W2", (E, HE, D)); inp("W3", (E, D, HE))
    inp("Ws1", (D, HS)); inp("Ws2", (HS, D)); inp("Ws3", (D, HS))
    out = nc.dram_tensor("out", [T, D], F32, kind="ExternalOutput").ap()
    dbg = {}
    if DEBUG:
        for nm, shape, dt in (("d_s", [T, D], BF16), ("d_xn", [T, D], BF16),
                              ("d_gidx", [NP + 128, 4], U32),
                              ("d_pair", [NP + 128, D], BF16),
                              ("d_w", [128, NT * 4], F32),
                              ("d_pos", [128, NT * 4], U32),
                              ("d_hfm", [8, 128, T], BF16),
                              ("d_ot", [NT, 8, 128, 128], BF16)):
            dbg[nm] = nc.dram_tensor(nm, shape, dt,
                                     kind="ExternalOutput").ap()

    xn_tm = nc.dram_tensor("xn_tm", [T, D], BF16, kind="Internal").ap()
    h_tm = nc.dram_tensor("h_tm", [T, D], BF16, kind="Internal").ap()
    xn_fm = nc.dram_tensor("xn_fm", [8, 128, T], BF16, kind="Internal").ap()
    hfm_d = nc.dram_tensor("hfm_d", [8, 128, T], BF16, kind="Internal").ap()
    ot = nc.dram_tensor("ot", [NT, 8, 128, 128], BF16, kind="Internal").ap()
    s_tm = nc.dram_tensor("s_tm", [T, D], BF16, kind="Internal").ap()
    gidx = nc.dram_tensor("gidx", [NP + 128, 4], U32, kind="Internal").ap()
    pairout = nc.dram_tensor("pairout", [NP + 128, D], BF16,
                             kind="Internal").ap()

    with ExitStack() as g_stk:
        gen = g_stk.enter_context
        em = Em(nc, g_stk)

        rb_np = nc.alloc_register(mybir.EngineType.Pool, "bnd_np")
        rb_t = nc.alloc_register(mybir.EngineType.Pool, "bnd_t")
        em.op("g", E_("reg_mov", rb_np, NP + 127), tick=False)
        em.op("g", E_("reg_mov", rb_t, T - 1), tick=False)

        c_ones_bf = gen(nc.sbuf_tensor("c_ones_bf", [1, 512], BF16))
        c_ones_f = gen(nc.sbuf_tensor("c_ones_f", [1, 128], F32))
        w_all = gen(nc.sbuf_tensor("w_all", [128, NT * 4], F32))
        pos_all = gen(nc.sbuf_tensor("pos_all", [128, NT * 4], U32))

        def ld_const(nm, dst, q="y", rearr=None, tag=None):
            src = ap[nm] if rearr is None else ap[nm].rearrange(rearr, p=128)
            em.op(q, E_("dma_start", out=dst[:], in_=src),
                  writes=[tag or (nm + "_c")], dma=True)

        with ExitStack() as stk:
            en = stk.enter_context
            xb = [en(nc.sbuf_tensor(f"xb{i}", [128, D], F32)) for i in range(2)]
            pb = [en(nc.sbuf_tensor(f"pb{i}", [128, D], F32)) for i in range(2)]
            xnb = [en(nc.sbuf_tensor(f"xnb{i}", [128, D], F32)) for i in range(2)]
            hb = [en(nc.sbuf_tensor(f"hb{i}", [128, D], F32)) for i in range(2)]
            hbfb = [en(nc.sbuf_tensor(f"hbfb{i}", [128, D], BF16))
                    for i in range(2)]
            xtb = [en(nc.sbuf_tensor(f"xtb{i}", [128, D], F32)) for i in range(2)]
            xtbfb = [en(nc.sbuf_tensor(f"xtbfb{i}", [128, D], BF16))
                     for i in range(2)]
            htb = [en(nc.sbuf_tensor(f"htb{i}", [128, D], BF16))
                   for i in range(2)]
            sqs = en(nc.sbuf_tensor("sqs", [128, D], F32))
            stt = [en(nc.sbuf_tensor(f"stt{i}", [128, 12], F32))
                   for i in range(2)]
            scf = [en(nc.sbuf_tensor(f"scf{i}", [128, E], F32)) for i in range(2)]
            soft = en(nc.sbuf_tensor("soft", [128, E], F32))
            top8 = [en(nc.sbuf_tensor(f"top8_{i}", [128, 8], F32))
                    for i in range(2)]
            idx8 = en(nc.sbuf_tensor("idx8", [128, 8], U32))
            idxf = [en(nc.sbuf_tensor(f"idxf{i}", [128, 8], F32))
                    for i in range(2)]
            oh = en(nc.sbuf_tensor("oh", [128, E], F32))
            mbf = en(nc.sbuf_tensor("mbf", [128, E], BF16))
            rank_sb = [en(nc.sbuf_tensor(f"rank_sb{i}", [128, E], F32))
                       for i in range(2)]
            base_sb = en(nc.sbuf_tensor("base_sb", [128, E], F32))
            rtmp = en(nc.sbuf_tensor("rtmp", [128, 12], F32))
            w4e = en(nc.sbuf_tensor("w4e", [128, 4], F32))
            dump4_sb = en(nc.sbuf_tensor("dump4_sb", [128, 4], F32))
            zgid = en(nc.sbuf_tensor("zgid", [128, 64, 4], U32))
            zpo = en(nc.sbuf_tensor("zpo", [128, D], BF16))
            c_u128 = en(nc.sbuf_tensor("c_u128", [128, 128], BF16))
            c_allones = en(nc.sbuf_tensor("c_allones", [128, 128], BF16))
            c_tok4 = en(nc.sbuf_tensor("c_tok4", [128, 4 * NT], U32))
            ps32 = [en(nc.psum_tensor(f"ps32_{i}", [128, D], F32))
                    for i in range(2)]
            psbf = [en(nc.psum_tensor(f"psbf_{i}", [128, D], BF16))
                    for i in range(2)]
            psr = en(nc.psum_tensor("psr", [128, E], F32))
            psc = en(nc.psum_tensor("psc", [128, 2 * E], F32))
            c_ident = en(nc.sbuf_tensor("c_ident", [128, 128], F32))
            c_identbf = en(nc.sbuf_tensor("c_identbf", [128, 128], BF16))
            c_iota = en(nc.sbuf_tensor("c_iota", [128, 32], F32))
            wg_sb = en(nc.sbuf_tensor("wg_sb", [128, 8, E], F32))

            ld_const("ident", c_ident)
            ld_const("identbf", c_identbf)
            ld_const("iota32", c_iota)
            ld_const("u128", c_u128)
            ld_const("allones", c_allones)
            ld_const("tok4", c_tok4)
            ld_const("Wg", wg_sb, rearr="(c p) e -> p c e", tag="wg_sb")
            em.op("v", E_("memset", c_ones_bf[:], 1.0), writes=["ones_bf"])
            em.op("v", E_("memset", c_ones_f[:], 1.0), writes=["ones_f"])
            em.op("v", E_("memset", base_sb[:], 0.0), writes=["base_sb"])
            ld_const("dumpcol4", dump4_sb, tag="dump4_c")
            em.op("v", E_("memset", zgid[:], 0), writes=["zgid"])
            em.op("v", E_("memset", zpo[:], 0.0), writes=["zpo"])
            NZR = (NP + 128) // 128
            for z0 in range(0, NZR, 64):
                zn = min(64, NZR - z0)
                em.op("g", E_("dma_start",
                              out=gidx[z0 * 128:(z0 + zn) * 128, :].rearrange(
                                  "(a p) o -> p a o", p=128),
                              in_=zgid[:, 0:zn, :]),
                      reads=["zgid"], writes=[f"gidx_z{z0}"], dma=True)
            em.op("g", E_("dma_start", out=pairout[NP:NP + 128, :],
                          in_=zpo[:]),
                  reads=["zpo"], dma=True)

            def ph0_load(i):
                j = i % 2
                r0, r1 = i * 128, (i + 1) * 128
                em.op("y", E_("dma_start", out=xb[j][:], in_=ap["x"][r0:r1, :]),
                      writes=[f"x{j}"], dma=True)
                em.op("y", E_("dma_start", out=pb[j][:],
                              in_=ap["pe_tm"][r0:r1, :]),
                      writes=[f"pe{j}"], dma=True)

            def stage2(k):
                j = k % 2
                nst = f"st{j}"
                em.op("s", E_("activation", w4e[:], top8[j][:, 0:4], AF.Exp,
                              bias=stt[j][:, 8:9]),
                      reads=[f"top8_{j}", nst + "m"], writes=["w4e"])
                em.op("v", E_("tensor_single_scalar", w4e[:], w4e[:],
                              stt[j][:, 10:11], ALU.mult),
                      reads=["w4e", nst + "rd"], writes=["w4e"])
                for jj in range(4):
                    em.op("v", E_("tensor_single_scalar", oh[:], c_iota[:],
                                  idxf[j][:, jj:jj + 1], ALU.is_equal),
                          reads=["iota32_c", f"idxf{j}"], writes=["oh"])
                    em.op("v", E_("tensor_mul", oh[:], oh[:], rank_sb[j][:]),
                          reads=["oh", f"rank{j}"], writes=["oh"])
                    em.op("v", E_("tensor_reduce", rtmp[:, jj:jj + 1], oh[:],
                                  AX.X, ALU.add),
                          reads=["oh"], writes=["rtmp"])
                em.op("v", E_("tensor_scalar_mul", rtmp[:, 4:8],
                              idxf[j][:, 0:4], float(TMAX * 128)),
                      reads=[f"idxf{j}"], writes=["rtmp"])
                em.op("v", E_("tensor_add", rtmp[:, 4:8], rtmp[:, 4:8],
                              rtmp[:, 0:4]),
                      reads=["rtmp"], writes=["rtmp"])
                em.op("v", E_("tensor_single_scalar", rtmp[:, 8:12],
                              rtmp[:, 0:4], float(TMAX * 128), ALU.is_lt),
                      reads=["rtmp"], writes=["rtmp"])
                em.op("v", E_("tensor_mul", w_all[:, 4 * k:4 * k + 4],
                              w4e[:], rtmp[:, 8:12]),
                      reads=["w4e", "rtmp"], writes=["w_all"])
                em.op("v", E_("scalar_tensor_tensor", rtmp[:, 4:8],
                              rtmp[:, 4:8], dump4_sb[:, 0:1], rtmp[:, 8:12],
                              ALU.subtract, ALU.mult),
                      reads=["rtmp", "dump4_c"], writes=["rtmp"])
                em.op("v", E_("tensor_single_scalar", rtmp[:, 4:8],
                              rtmp[:, 4:8], dump4_sb[:, 0:1], ALU.add),
                      reads=["rtmp", "dump4_c"], writes=["rtmp"])
                em.op("v", E_("tensor_copy", pos_all[:, 4 * k:4 * k + 4],
                              rtmp[:, 4:8]),
                      reads=["rtmp"], writes=["pos_all"])
                if NOSCAT:
                    return
                for jj in range(4):
                    pj = 4 * k + jj
                    em.op("g", E_("indirect_dma_start", out=gidx,
                                  out_offset=IndirectOffsetOnAxis(
                                      pos_all[:, pj:pj + 1], axis=0),
                                  in_=c_tok4[:, 4 * k:4 * k + 4],
                                  in_offset=None, bounds_check=rb_np,
                                  oob_is_err=False),
                          reads=["pos_all", "tok4_c"]
                          + [f"gidx_z{z0}" for z0 in range(0, 257, 64)],
                          dma=True)

            ph0_load(0)
            for i in range(NT):
                j = i % 2
                r0, r1 = i * 128, (i + 1) * 128
                if i + 1 < NT:
                    ph0_load(i + 1)
                nx, np_, nxn, nh = f"x{j}", f"pe{j}", f"xn{j}", f"h{j}"
                nhbf, nst, nxt = f"hbf{j}", f"st{j}", f"xt{j}"
                nxtbf, nht = f"xtbf{j}", f"ht{j}"
                st = stt[j]
                em.op("v", E_("tensor_reduce", st[:, 0:1], xb[j][:], AX.X,
                              ALU.add), reads=[nx], writes=[nst])
                em.op("s", E_("activation", sqs[:], xb[j][:], AF.Square,
                              accum_out=st[:, 1:2]),
                      reads=[nx], writes=["sqs", nst + "q"])
                em.op("v", E_("tensor_scalar_mul", st[:, 2:3], st[:, 0:1],
                              1.0 / D), reads=[nst], writes=[nst])
                em.op("v", E_("tensor_mul", st[:, 3:4], st[:, 2:3],
                              st[:, 2:3]),
                      reads=[nst], writes=[nst])
                if i >= 1:
                    stage2(i - 1)
                em.op("v", E_("tensor_scalar", st[:, 4:5], st[:, 1:2],
                              1.0 / D, EPS, ALU.mult, ALU.add),
                      reads=[nst, nst + "q"], writes=[nst])
                em.op("v", E_("tensor_sub", st[:, 4:5], st[:, 4:5],
                              st[:, 3:4]),
                      reads=[nst], writes=[nst])
                em.op("s", E_("activation", st[:, 5:6], st[:, 4:5], AF.Sqrt),
                      reads=[nst], writes=[nst + "s"])
                em.op("v", E_("reciprocal", st[:, 6:7], st[:, 5:6]),
                      reads=[nst + "s"], writes=[nst + "r"])
                em.op("v", E_("tensor_scalar", xnb[j][:], xb[j][:],
                              st[:, 2:3], st[:, 6:7], ALU.subtract,
                              ALU.mult),
                      reads=[nx, nst, nst + "r"], writes=[nxn])
                em.op("v", E_("tensor_add", hb[j][:], xnb[j][:], pb[j][:]),
                      reads=[nxn, np_], writes=[nh])
                em.op("s", E_("activation", hbfb[j][:], hb[j][:], AF.Copy),
                      reads=[nh], writes=[nhbf])
                em.op("g", E_("dma_start", out=xn_tm[r0:r1, :], in_=xnb[j][:]),
                      reads=[nxn], dma=True)
                em.op("y", E_("dma_start", out=h_tm[r0:r1, :], in_=hbfb[j][:]),
                      reads=[nhbf], dma=True)
                for c in range(8):
                    em.op("t", E_("transpose", ps32[j][:, c * 128:(c + 1) * 128],
                                  xnb[j][:, c * 128:(c + 1) * 128], c_ident[:]),
                          reads=[nxn, "ident_c"], writes=[f"ps32_{j}"],
                          tick=(c == 7))
                    em.op("t", E_("transpose", psbf[j][:, c * 128:(c + 1) * 128],
                                  hbfb[j][:, c * 128:(c + 1) * 128],
                                  c_identbf[:]),
                          reads=[nhbf, "identbf_c"], writes=[f"psbf_{j}"],
                          tick=(c == 7))
                em.op("v", E_("tensor_copy", xtb[j][:], ps32[j][:]),
                      reads=[f"ps32_{j}"], writes=[nxt])
                em.op("v", E_("tensor_copy", xtbfb[j][:], ps32[j][:]),
                      reads=[f"ps32_{j}"], writes=[nxtbf])
                em.op("v", E_("tensor_copy", htb[j][:], psbf[j][:]),
                      reads=[f"psbf_{j}"], writes=[nht])
                em.op("g", E_("dma_start",
                              out=xn_fm[:, :, r0:r1].rearrange("c p t -> p c t"),
                              in_=xtbfb[j].ap().rearrange("p (c t) -> p c t",
                                                          c=8)),
                      reads=[nxtbf], dma=True)
                em.op("y", E_("dma_start",
                              out=hfm_d[:, :, r0:r1].rearrange("c p t -> p c t"),
                              in_=htb[j].ap().rearrange("p (c t) -> p c t",
                                                        c=8)),
                      reads=[nht], dma=True)
                for c in range(8):
                    em.op("t", E_("matmul", psr[:],
                                  xtb[j][:, c * 128:(c + 1) * 128],
                                  wg_sb[:, c, :], start=(c == 0), stop=(c == 7)),
                          reads=[nxt, "wg_sb"], writes=["psr"], tick=(c == 7))
                em.op("v", E_("tensor_copy", scf[j][:], psr[:]),
                      reads=["psr"], writes=[f"sc{j}"])
                em.op("v", E_("tensor_reduce", st[:, 8:9], scf[j][:], AX.X,
                              ALU.max, negate=True),
                      reads=[f"sc{j}"], writes=[nst + "m"])
                em.op("s", E_("activation", soft[:], scf[j][:], AF.Exp,
                              bias=st[:, 8:9], accum_out=st[:, 9:10]),
                      reads=[f"sc{j}", nst + "m"], writes=["soft", nst + "d"])
                em.op("v", E_("reciprocal", st[:, 10:11], st[:, 9:10]),
                      reads=[nst + "d"], writes=[nst + "rd"])
                em.op("v", E_("max", top8[j][:], scf[j][:]),
                      reads=[f"sc{j}"], writes=[f"top8_{j}"])
                em.op("v", E_("max_index", idx8[:], top8[j][:], scf[j][:]),
                      reads=[f"top8_{j}", f"sc{j}"], writes=["idx8"])
                em.op("v", E_("tensor_copy", idxf[j][:], idx8[:]),
                      reads=["idx8"], writes=[f"idxf{j}"])
                em.op("v", E_("tensor_single_scalar", oh[:], scf[j][:],
                              top8[j][:, 3:4], ALU.is_ge),
                      reads=[f"sc{j}", f"top8_{j}"], writes=["oh"])
                em.op("s", E_("activation", mbf[:], oh[:], AF.Copy),
                      reads=["oh"], writes=["mbf"])
                em.op("t", E_("matmul", psc[:, 0:E], c_u128[:], mbf[:],
                              start=True, stop=True),
                      reads=["u128_c", "mbf"], writes=["psc_r"])
                em.op("t", E_("matmul", psc[:, E:2 * E], c_allones[:], mbf[:],
                              start=True, stop=True),
                      reads=["allones_c", "mbf"], writes=["psc_t"])
                em.op("v", E_("tensor_add", rank_sb[j][:], psc[:, 0:E],
                              base_sb[:]),
                      reads=["psc_r", "base_sb"], writes=[f"rank{j}"])
                em.op("v", E_("tensor_add", base_sb[:], base_sb[:],
                              psc[:, E:2 * E]),
                      reads=["base_sb", "psc_t"], writes=["base_sb"])
            stage2(NT - 1)
            em.new_phase()

        with ExitStack() as stk:
            en = stk.enter_context
            hfm = en(nc.sbuf_tensor("hfm", [128, 8, T], BF16))
            qg = en(nc.sbuf_tensor("qg", [128, T], BF16))
            kg = en(nc.sbuf_tensor("kg", [128, T], BF16))
            vkm = en(nc.sbuf_tensor("vkm", [128, TC, 130], BF16))
            wqs = en(nc.sbuf_tensor("wqs", [128, 8, 128], BF16))
            wks = en(nc.sbuf_tensor("wks", [128, 8, 128], BF16))
            wvs = en(nc.sbuf_tensor("wvs", [128, 8, 128], BF16))
            qsq = [en(nc.sbuf_tensor(f"qsq{i}", [128, 512], BF16))
                   for i in range(2)]
            nst2 = en(nc.sbuf_tensor("nst2", [2, 8], F32))
            negm = en(nc.sbuf_tensor("negm", [128, 2], F32))
            qn_sb = en(nc.sbuf_tensor("qn_sb", [2, 512], F32))
            expst = [[en(nc.sbuf_tensor(f"expst{h}_{i}", [128, 512], BF16))
                      for i in range(EXPD)] for h in range(2)]
            l_bf = [en(nc.sbuf_tensor(f"l_bf{i}", [1, 512], BF16))
                    for i in range(2)]
            rb_sb = [en(nc.sbuf_tensor(f"rb_sb{i}", [64, 512], F32))
                     for i in range(2)]
            ost = [en(nc.sbuf_tensor(f"ost{i}", [64, 512], BF16))
                   for i in range(2)]
            c_oc2 = en(nc.sbuf_tensor("c_oc2", [128, 2], BF16))
            c_sel2 = en(nc.sbuf_tensor("c_sel2", [2, 256], F32))
            b_q = en(nc.sbuf_tensor("b_q", [1, D], BF16))
            b_k = en(nc.sbuf_tensor("b_k", [1, D], BF16))
            b_v = en(nc.sbuf_tensor("b_v", [1, D], BF16))
            w1s = en(nc.sbuf_tensor("w1s", [128, 8, HE], BF16))
            w3s = en(nc.sbuf_tensor("w3s", [128, 8, HE], BF16))
            w2s = en(nc.sbuf_tensor("w2s", [128, 4, D], BF16))
            gi_sb = [en(nc.sbuf_tensor(f"gi_sb{i}", [128, 4, 4], U32))
                     for i in range(2)]
            xg = [en(nc.sbuf_tensor(f"xg{i}", [128, 4, D], BF16))
                  for i in range(2)]
            xgt = [en(nc.sbuf_tensor(f"xgt{i}", [128, 8, 512], BF16))
                   for i in range(2)]
            asil = en(nc.sbuf_tensor("asil", [128, 4, 512], BF16))
            silt = en(nc.sbuf_tensor("silt", [128, 512], BF16))
            po_sb = [en(nc.sbuf_tensor(f"po_sb{i}", [128, D], BF16))
                     for i in range(2)]
            c_idbf2 = en(nc.sbuf_tensor("c_idbf2", [128, 128], BF16))
            bank = [en(nc.psum_tensor(f"bank{i}", [128, 512], F32))
                    for i in range(6)]
            psT = en(nc.psum_tensor("psT", [128, 1024], BF16))
            psX = en(nc.psum_tensor("psX", [128, 512], F32))

            def wload(g):
                for nm, dst, w in (("wqs", wqs, "Wq"), ("wks", wks, "Wk"),
                                   ("wvs", wvs, "Wv")):
                    em.op("g", E_("dma_start", out=dst[:],
                                  in_=ap[w][:, g * 128:(g + 1) * 128].rearrange(
                                      "(c p) n -> p c n", p=128)),
                          writes=[nm], dma=True)

            def a1_gen():
                ld_const("onescol2", c_oc2)
                ld_const("sel2", c_sel2)
                for nm, dst in (("bq", b_q), ("bk", b_k), ("bv", b_v)):
                    ld_const(nm, dst, q="g", tag=nm)
                for c in range(8):
                    em.op("y", E_("dma_start", out=hfm[:, c, :], in_=hfm_d[c]),
                          writes=["hfm"], dma=True)
                em.op("v", E_("memset", vkm[:], 0.0), writes=["vkm"])
                em.op("v", E_("memset", vkm[:, :, 64:65], 1.0), writes=["vkm1"])
                em.op("v", E_("memset", vkm[:, :, 129:130], 1.0),
                      writes=["vkm1"])
                wload(0)
                yield 0
                for g in range(NG):
                    gsl = slice(g * 128, (g + 1) * 128)
                    for (dst, wsb, wn, brow, bn, bks) in (
                            (qg, wqs, "wqs", b_q, "bq", (0, 2)),
                            (kg, wks, "wks", b_k, "bk", (1, 3))):
                        for sb in range(SB):
                            ssl = slice(sb * 512, (sb + 1) * 512)
                            bi = bks[sb % 2]
                            ps, pn = bank[bi], f"bk{bi}"
                            for c in range(8):
                                em.op("t", E_("matmul", ps[:], wsb[:, c, :],
                                              hfm[:, c, ssl], start=(c == 0),
                                              stop=False),
                                      reads=[wn, "hfm"], writes=[pn],
                                      tick=(c == 7))
                            em.op("t", E_("matmul", ps[:], brow[:, gsl],
                                          c_ones_bf[:], start=False,
                                          stop=True),
                                  reads=[bn, "ones_bf"], writes=[pn])
                            em.op("v", E_("tensor_copy", dst[:, ssl], ps[:]),
                                  reads=[pn], writes=[wn + "o"])
                            yield 2000
                    for i in range(NT):
                        tsl = slice(i * 128, (i + 1) * 128)
                        bi = (2, 5)[i % 2]
                        psv, pvn = bank[bi], f"bk{bi}"
                        for c in range(8):
                            em.op("t", E_("matmul", psv[:, 0:128],
                                          hfm[:, c, tsl],
                                          wvs[:, c, :], start=(c == 0),
                                          stop=False),
                                  reads=["wvs", "hfm"], writes=[pvn],
                                  tick=(c == 7))
                        em.op("t", E_("matmul", psv[:, 0:128],
                                      c_ones_bf[:, 0:128],
                                      b_v[:, gsl], start=False, stop=True),
                              reads=["bv", "ones_bf"], writes=[pvn])
                        em.op("v", E_("tensor_copy", vkm[:, i, 0:64],
                                      psv[:, 0:64]),
                              reads=[pvn], writes=["vkm"])
                        em.op("v", E_("tensor_copy", vkm[:, i, 65:129],
                                      psv[:, 64:128]),
                              reads=[pvn], writes=["vkm"])
                        yield 1100
                    if g + 1 < NG:
                        wload(g + 1)
                    for qk, (src, srcn) in enumerate(((qg, "wqso"),
                                                      (kg, "wkso"))):
                        col = slice(qk, qk + 1)
                        em.op("v", E_("memset", nst2[:, col], 0.0),
                              writes=["nst2"])

                        def mstep(sb, col=col):
                            bi = (3, 0)[sb % 2]
                            em.op("t", E_("matmul", bank[bi][0:2, :],
                                          c_oc2[:], qsq[sb % 2][:],
                                          start=True, stop=True),
                                  reads=["onescol2_c", f"qsq{sb % 2}"],
                                  writes=[f"bk{bi}"])
                            em.op("v", E_("tensor_reduce", qn_sb[:, 0:1],
                                          bank[bi][0:2, :], AX.X, ALU.max),
                                  reads=[f"bk{bi}"], writes=["qn_sb"])
                            em.op("v", E_("tensor_max", nst2[:, col],
                                          nst2[:, col], qn_sb[:, 0:1]),
                                  reads=["nst2", "qn_sb"], writes=["nst2"])

                        for sb in range(SB):
                            ssl = slice(sb * 512, (sb + 1) * 512)
                            em.op("s", E_("activation", qsq[sb % 2][:],
                                          src[:, ssl], AF.Square),
                                  reads=[srcn], writes=[f"qsq{sb % 2}"])
                            if sb >= 1:
                                mstep(sb - 1)
                            yield 300
                        mstep(SB - 1)
                        yield 300
                    em.op("v", E_("tensor_mul", nst2[:, 2:3], nst2[:, 0:1],
                                  nst2[:, 1:2]), reads=["nst2"],
                          writes=["nst2"])
                    em.op("s", E_("activation", nst2[:, 3:4], nst2[:, 2:3],
                                  AF.Sqrt), reads=["nst2"], writes=["nst2s"])
                    em.op("v", E_("tensor_scalar_mul", nst2[:, 4:5],
                                  nst2[:, 3:4], -0.125 * 1.02),
                          reads=["nst2s"], writes=["nst2m"])
                    for hh in range(2):
                        em.op("t", E_("matmul", bank[4][:, 0:1],
                                      c_sel2[:, hh * 128:(hh + 1) * 128],
                                      nst2[:, 4:5], start=True, stop=True),
                              reads=["sel2_c", "nst2m"], writes=["bk4"])
                        em.op("v", E_("tensor_copy", negm[:, hh:hh + 1],
                                      bank[4][:, 0:1]), reads=["bk4"],
                              writes=["negm"])
                    yield 400
                    for sb in range(SB):
                        s0 = sb * 512
                        qsl = slice(s0, s0 + 512)
                        for tcx in range(TC + 3):
                            if tcx >= 3:
                                tc = tcx - 3
                                ej = tc % EXPD
                                for hh in range(2):
                                    em.op("t", E_("matmul",
                                                  bank[4 + hh][0:65, :],
                                                  vkm[:, tc, 65 * hh:
                                                      65 * hh + 65],
                                                  expst[hh][ej][:],
                                                  start=(tc == 0),
                                                  stop=(tc == TC - 1)),
                                          reads=["vkm", "vkm1",
                                                 f"ex{hh}_{ej}"],
                                          writes=[f"bk{4 + hh}"], tick=True)
                                yield 700
                            if tcx < TC:
                                tc = tcx
                                tsl = slice(tc * 128, (tc + 1) * 128)
                                dj = (tc + sb) % 2
                                ej = tc % EXPD
                                em.op("t", E_("matmul", bank[dj][:],
                                              kg[0:64, tsl], qg[0:64, qsl],
                                              start=True, stop=True),
                                      reads=["wkso", "wqso"],
                                      writes=[f"bk{dj}"], tick=True)
                                em.op("t", E_("matmul", bank[2 + dj][:],
                                              kg[64:128, tsl],
                                              qg[64:128, qsl],
                                              start=True, stop=True),
                                      reads=["wkso", "wqso"],
                                      writes=[f"bk{2 + dj}"], tick=True)
                                for hh in range(2):
                                    em.op("s", E_("activation",
                                                  expst[hh][ej][:],
                                                  bank[2 * hh + dj][:],
                                                  AF.Exp,
                                                  bias=negm[:, hh:hh + 1],
                                                  scale=0.125),
                                          reads=[f"bk{2 * hh + dj}", "negm"],
                                          writes=[f"ex{hh}_{ej}"])
                                yield 600
                        for hh in range(2):
                            em.op("v", E_("tensor_copy", l_bf[hh][:],
                                          bank[4 + hh][64:65, :]),
                                  reads=[f"bk{4 + hh}"],
                                  writes=[f"l_bf{hh}"])
                        yield 300
                        for hh in range(2):
                            em.op("t", E_("matmul", bank[4 + hh][64:128, :],
                                          c_ones_bf[:, 0:64], l_bf[hh][:],
                                          start=True, stop=True),
                                  reads=["ones_bf", f"l_bf{hh}"],
                                  writes=[f"bk{4 + hh}"], tick=True)
                            em.op("v", E_("reciprocal", rb_sb[hh][:],
                                          bank[4 + hh][64:128, :]),
                                  reads=[f"bk{4 + hh}"],
                                  writes=[f"rb_sb{hh}"])
                        yield 300
                        for hh in range(2):
                            gh = 2 * g + hh
                            em.op("v", E_("tensor_mul", ost[hh][:],
                                          bank[4 + hh][0:64, :],
                                          rb_sb[hh][:]),
                                  reads=[f"bk{4 + hh}", f"rb_sb{hh}"],
                                  writes=[f"ost{hh}"])
                            em.op("y", E_("dma_start",
                                          out=ot[sb * 4:(sb + 1) * 4,
                                                 gh // 2,
                                                 64 * (gh % 2):
                                                 64 * (gh % 2) + 64,
                                                 :].rearrange(
                                              "a r x -> r a x"),
                                          in_=ost[hh].ap().rearrange(
                                              "r (a x) -> r a x", a=4)),
                                  reads=[f"ost{hh}"], dma=True)
                            yield 500

            NST = E * 2

            def bload(e):
                for nm, dst, w in (("w1s", w1s, "W1"), ("w3s", w3s, "W3"),
                                   ("w2s", w2s, "W2")):
                    em.op("g", E_("dma_start", out=dst[:],
                                  in_=ap[w][e].rearrange("(c p) h -> p c h",
                                                         p=128)),
                          writes=[nm], dma=True)

            def gload(st_):
                sj = st_ % 2
                base = st_ * 512
                em.op("y", E_("dma_start", out=gi_sb[sj][:],
                              in_=gidx[base:base + 512, :].rearrange(
                                  "(a p) o -> p a o", p=128)),
                      writes=[f"gi{sj}"], dma=True)
                for sub in range(4):
                    em.op("g", E_("indirect_dma_start", out=xg[sj][:, sub, :],
                                  out_offset=None, in_=xn_tm,
                                  in_offset=IndirectOffsetOnAxis(
                                      gi_sb[sj][:, sub, 0:1], axis=0),
                                  bounds_check=rb_t, oob_is_err=False),
                          reads=[f"gi{sj}"], writes=[f"xg{sj}"], dma=True)

            def trs(st_, c):
                sj = st_ % 2
                for sub in range(4):
                    em.op("t", E_("transpose",
                                  psT[:, sub * 128:(sub + 1) * 128],
                                  xg[sj][:, sub,
                                         c * 128:(c + 1) * 128],
                                  c_idbf2[:]),
                          reads=[f"xg{sj}", "idbf2_c"],
                          writes=["psT"], tick=(sub == 3))
                em.op("v", E_("tensor_copy", xgt[sj][:, c, :],
                              psT[:, 0:512]),
                      reads=["psT"], writes=[f"xgt{sj}"])

            def b_gen():
                ld_const("identbf", c_idbf2, tag="idbf2_c")
                bload(0)
                gload(0)
                gload(1)
                yield 0
                for c in range(8):
                    trs(0, c)
                    yield 700
                for st_ in range(NST):
                    e = st_ // 2
                    sj = st_ % 2
                    base = st_ * 512
                    if st_ + 2 < NST:
                        gload(st_ + 2)
                    for hc in range(4):
                        hsl = slice(hc * 128, (hc + 1) * 128)
                        for c in range(8):
                            em.op("t", E_("matmul", psX[:], w1s[:, c, hsl],
                                          xgt[sj][:, c, :], start=(c == 0),
                                          stop=(c == 7)),
                                  reads=["w1s", f"xgt{sj}"], writes=["psX"],
                                  tick=(c == 7))
                        em.op("s", E_("activation", silt[:], psX[:], AF.Silu),
                              reads=["psX"], writes=["silt"])
                        yield 2100
                        if st_ + 1 < NST:
                            trs(st_ + 1, 2 * hc)
                            yield 700
                        for c in range(8):
                            em.op("t", E_("matmul", psX[:], w3s[:, c, hsl],
                                          xgt[sj][:, c, :], start=(c == 0),
                                          stop=(c == 7)),
                                  reads=["w3s", f"xgt{sj}"], writes=["psX"],
                                  tick=(c == 7))
                        em.op("v", E_("tensor_mul", asil[:, hc, :], silt[:],
                                      psX[:]),
                              reads=["silt", "psX"], writes=["asil"])
                        yield 2100
                        if st_ + 1 < NST:
                            trs(st_ + 1, 2 * hc + 1)
                            yield 700
                    for tt in range(4):
                        tsl = slice(tt * 128, (tt + 1) * 128)
                        oj = tt % 2
                        for half in range(2):
                            dsl = slice(half * 512, (half + 1) * 512)
                            for hc in range(4):
                                em.op("t", E_("matmul", psX[:],
                                              asil[:, hc, tsl],
                                              w2s[:, hc, dsl],
                                              start=(hc == 0), stop=(hc == 3)),
                                      reads=["asil", "w2s"],
                                      writes=["psX"], tick=(hc == 3))
                            em.op("v", E_("tensor_copy", po_sb[oj][:, dsl],
                                          psX[:]),
                                  reads=["psX"], writes=[f"po{oj}"])
                            yield 1100
                        em.op("y", E_("dma_start",
                                      out=pairout[base + tt * 128:
                                                  base + (tt + 1) * 128, :],
                                      in_=po_sb[oj][:]),
                              reads=[f"po{oj}"], dma=True)
                    if st_ % 2 == 1 and e + 1 < E:
                        bload(e + 1)

            ga, gb = a1_gen(), b_gen()
            TA, TB = 3.51e6, 2.0e6
            pa, pb = 0.0, 0.0
            da = db = False
            pb += next(gb)
            while not (da and db):
                if db or (not da and pa * TB <= pb * TA):
                    try:
                        pa += next(ga)
                    except StopIteration:
                        da = True
                else:
                    try:
                        pb += next(gb)
                    except StopIteration:
                        db = True
            em.new_phase()

        with ExitStack() as stkO:
            with ExitStack() as stk2:
                en2 = stk2.enter_context
                wo_sb = en2(nc.sbuf_tensor("wo_sb", [128, 8, D], BF16))
                otile = [en2(nc.sbuf_tensor(f"otile{i}", [128, 8, 128], BF16))
                         for i in range(2)]
                htb2 = [en2(nc.sbuf_tensor(f"htb2{i}", [128, D], BF16))
                        for i in range(2)]
                yb = [en2(nc.sbuf_tensor(f"yb{i}", [128, D], F32))
                      for i in range(2)]
                sb_t = [en2(nc.sbuf_tensor(f"sb_t{i}", [128, D], F32))
                        for i in range(2)]
                sqs2 = en2(nc.sbuf_tensor("sqs2", [128, D], F32))
                st2 = [en2(nc.sbuf_tensor(f"st2_{i}", [128, 8], F32))
                       for i in range(2)]
                b_o = en2(nc.sbuf_tensor("b_o", [1, D], BF16))
                gam_r = en2(nc.sbuf_tensor("gam_r", [1, D], F32))
                bet_r = en2(nc.sbuf_tensor("bet_r", [1, D], F32))
                gam_b = en2(nc.sbuf_tensor("gam_b", [128, D], F32))
                bet_b = en2(nc.sbuf_tensor("bet_b", [128, D], F32))
                xnsb = [en2(nc.sbuf_tensor(f"xnsb{i}", [128, 8, 512], BF16))
                        for i in range(2)]
                pg = [en2(nc.sbuf_tensor(f"pg{i}", [128, 4, D], BF16))
                      for i in range(2)]
                ymt = [en2(nc.sbuf_tensor(f"ymt{i}", [128, D], F32))
                       for i in range(2)]
                ws1 = en2(nc.sbuf_tensor("ws1", [128, 8, HS], BF16))
                ws3_ = en2(nc.sbuf_tensor("ws3_", [128, 8, HS], BF16))
                ws2_ = en2(nc.sbuf_tensor("ws2_", [128, 8, D], BF16))
                wact = [en2(nc.sbuf_tensor(f"wact{i}", [128, 8, 512], BF16))
                        for i in range(2)]
                silt2 = en2(nc.sbuf_tensor("silt2", [128, 512], BF16))
                gbuf = en2(nc.sbuf_tensor("gbuf", [128, 512], F32))
                xb2 = [en2(nc.sbuf_tensor(f"xb2_{i}", [128, D], F32))
                       for i in range(2)]
                ob = [en2(nc.sbuf_tensor(f"ob{i}", [128, D], F32))
                      for i in range(2)]
                psw = en2(nc.psum_tensor("psw", [128, 1024], F32))
                psu2 = [en2(nc.psum_tensor(f"psu2_{i}", [128, 512], F32))
                        for i in range(2)]
                psv2 = [en2(nc.psum_tensor(f"psv2_{i}", [128, 512], F32))
                        for i in range(2)]
                psz = [en2(nc.psum_tensor(f"psz{i}", [128, 512], F32))
                       for i in range(2)]

                em.op("g", E_("dma_start", out=wo_sb[:],
                              in_=ap["Wo"].rearrange("(c p) n -> p c n",
                                                     p=128)),
                      writes=["wo_sb"], dma=True)
                ld_const("bo", b_o, q="g", tag="bo")
                ld_const("gamma", gam_r)
                ld_const("beta", bet_r)
                for nm, dst, w in (("ws1", ws1, "Ws1"), ("ws3_", ws3_, "Ws3"),
                                   ("ws2_", ws2_, "Ws2")):
                    em.op("g", E_("dma_start", out=dst[:],
                                  in_=ap[w].rearrange("(c p) h -> p c h",
                                                      p=128)),
                          writes=[nm], dma=True)
                for src, srcn, dst, dstn in ((gam_r, "gamma_c", gam_b,
                                              "gam_b"),
                                             (bet_r, "beta_c", bet_b,
                                              "bet_b")):
                    for half in range(2):
                        hsl = slice(half * 512, (half + 1) * 512)
                        em.op("t", E_("matmul", psz[0][:], c_ones_f[:],
                                      src[:, hsl], start=True, stop=True),
                              reads=["ones_f", srcn], writes=["psz0"])
                        em.op("v", E_("tensor_copy", dst[:, hsl], psz[0][:]),
                              reads=["psz0"], writes=[dstn])

                def a2_load(i):
                    j = i % 2
                    r0, r1 = i * 128, (i + 1) * 128
                    em.op("y", E_("dma_start", out=otile[j][:],
                                  in_=ot[i].rearrange("c p t -> p c t")),
                          writes=[f"otile{j}"], dma=True)
                    em.op("y", E_("dma_start", out=htb2[j][:],
                                  in_=h_tm[r0:r1, :]),
                          writes=[f"ht2{j}"], dma=True)

                def c_load(ti):
                    j = ti % 2
                    r0, r1 = ti * 128, (ti + 1) * 128
                    em.op("y", E_("dma_start", out=xb2[j][:],
                                  in_=ap["x"][r0:r1, :]),
                          writes=[f"x2_{j}"], dma=True)
                    for jj in range(4):
                        em.op("g", E_("indirect_dma_start",
                                      out=pg[j][:, jj, :], out_offset=None,
                                      in_=pairout,
                                      in_offset=IndirectOffsetOnAxis(
                                          pos_all[:, ti * 4 + jj:
                                                  ti * 4 + jj + 1], axis=0),
                                      bounds_check=rb_np,
                                      oob_is_err=False),
                              reads=["pos_all"], writes=[f"pg{j}"], dma=True)

                def xn_load(sb):
                    j2 = sb % 2
                    ssl = slice(sb * 512, (sb + 1) * 512)
                    em.op("y", E_("dma_start", out=xnsb[j2][:],
                                  in_=xn_fm[:, :, ssl].rearrange(
                                      "c p t -> p c t")),
                          writes=[f"xnsb{j2}"], dma=True)

                def chid_unit(sb2, hc):
                    j2 = sb2 % 2
                    hj = hc % 2
                    hsl = slice(hc * 128, (hc + 1) * 128)
                    for c in range(8):
                        em.op("t", E_("matmul", psu2[hj][:],
                                      ws1[:, c, hsl],
                                      xnsb[j2][:, c, :], start=(c == 0),
                                      stop=(c == 7)),
                              reads=["ws1", f"xnsb{j2}"],
                              writes=[f"psu2_{hj}"], tick=(c == 7))
                    em.op("s", E_("activation", silt2[:], psu2[hj][:],
                                  AF.Silu), reads=[f"psu2_{hj}"],
                          writes=["silt2"])
                    for c in range(8):
                        em.op("t", E_("matmul", psv2[hj][:],
                                      ws3_[:, c, hsl],
                                      xnsb[j2][:, c, :], start=(c == 0),
                                      stop=(c == 7)),
                              reads=["ws3_", f"xnsb{j2}"],
                              writes=[f"psv2_{hj}"], tick=(c == 7))
                    em.op("v", E_("tensor_mul", wact[j2][:, hc, :], silt2[:],
                                  psv2[hj][:]),
                          reads=["silt2", f"psv2_{hj}"],
                          writes=[f"wact{j2}"])

                a2_load(0)
                c_load(0)
                xn_load(0)
                xn_load(1)
                for hc in range(8):
                    chid_unit(0, hc)
                for sb in range(SB):
                    j2 = sb % 2
                    if sb + 2 < SB:
                        xn_load(sb + 2)
                    for tt in range(4):
                        ti = sb * 4 + tt
                        j = ti % 2
                        r0, r1 = ti * 128, (ti + 1) * 128
                        tsl = slice(tt * 128, (tt + 1) * 128)
                        if ti + 1 < NT:
                            a2_load(ti + 1)
                            c_load(ti + 1)
                        for half in range(2):
                            dsl = slice(half * 512, (half + 1) * 512)
                            for c in range(8):
                                em.op("t", E_("matmul", psw[:, dsl],
                                              otile[j][:, c, :],
                                              wo_sb[:, c, dsl],
                                              start=(c == 0), stop=False),
                                      reads=[f"otile{j}", "wo_sb"],
                                      writes=["psw"], tick=(c == 7))
                            em.op("t", E_("matmul", psw[:, dsl],
                                          c_ones_bf[:, 0:128],
                                          b_o[:, dsl], start=False,
                                          stop=True),
                                  reads=["bo", "ones_bf"], writes=["psw"])
                        em.op("v", E_("tensor_add", yb[j][:], psw[:],
                                      htb2[j][:]),
                              reads=["psw", f"ht2{j}"], writes=[f"y{j}"])
                        st = st2[j]
                        em.op("v", E_("tensor_reduce", st[:, 0:1], yb[j][:],
                                      AX.X, ALU.add),
                              reads=[f"y{j}"], writes=[f"st2{j}"])
                        em.op("s", E_("activation", sqs2[:], yb[j][:],
                                      AF.Square, accum_out=st[:, 1:2]),
                              reads=[f"y{j}"], writes=["sqs2", f"st2{j}q"])
                        em.op("v", E_("tensor_scalar_mul", st[:, 2:3],
                                      st[:, 0:1], 1.0 / D),
                              reads=[f"st2{j}"], writes=[f"st2{j}"])
                        em.op("v", E_("tensor_mul", st[:, 3:4], st[:, 2:3],
                                      st[:, 2:3]),
                              reads=[f"st2{j}"], writes=[f"st2{j}"])
                        em.op("v", E_("tensor_scalar", st[:, 4:5], st[:, 1:2],
                                      1.0 / D, EPS, ALU.mult, ALU.add),
                              reads=[f"st2{j}", f"st2{j}q"],
                              writes=[f"st2{j}"])
                        em.op("v", E_("tensor_sub", st[:, 4:5], st[:, 4:5],
                                      st[:, 3:4]),
                              reads=[f"st2{j}"], writes=[f"st2{j}"])
                        em.op("s", E_("activation", st[:, 5:6], st[:, 4:5],
                                      AF.Sqrt),
                              reads=[f"st2{j}"], writes=[f"st2{j}s"])
                        em.op("v", E_("reciprocal", st[:, 6:7], st[:, 5:6]),
                              reads=[f"st2{j}s"], writes=[f"st2{j}r"])
                        em.op("v", E_("tensor_scalar", yb[j][:], yb[j][:],
                                      st[:, 2:3], st[:, 6:7], ALU.subtract,
                                      ALU.mult),
                              reads=[f"y{j}", f"st2{j}", f"st2{j}r"],
                              writes=[f"y{j}"])
                        em.op("v", E_("tensor_mul", yb[j][:], yb[j][:],
                                      gam_b[:]),
                              reads=[f"y{j}", "gam_b"], writes=[f"y{j}"])
                        em.op("v", E_("tensor_add", sb_t[j][:], yb[j][:],
                                      bet_b[:]),
                              reads=[f"y{j}", "bet_b"], writes=[f"s{j}"])
                        em.op("v", E_("tensor_single_scalar", ymt[j][:],
                                      pg[j][:, 0, :],
                                      w_all[:, ti * 4:ti * 4 + 1], ALU.mult),
                              reads=[f"pg{j}", "w_all"], writes=[f"ymt{j}"])
                        for jj in range(1, 4):
                            em.op("v", E_("scalar_tensor_tensor", ymt[j][:],
                                          pg[j][:, jj, :],
                                          w_all[:, ti * 4 + jj:
                                                ti * 4 + jj + 1],
                                          ymt[j][:], ALU.mult, ALU.add),
                                  reads=[f"pg{j}", "w_all", f"ymt{j}"],
                                  writes=[f"ymt{j}"])
                        if sb + 1 < SB:
                            chid_unit(sb + 1, 2 * tt)
                        for half in range(2):
                            dsl = slice(half * 512, (half + 1) * 512)
                            for hc in range(8):
                                em.op("t", E_("matmul", psz[half][:],
                                              wact[j2][:, hc, tsl],
                                              ws2_[:, hc, dsl],
                                              start=(hc == 0),
                                              stop=(hc == 7)),
                                      reads=[f"wact{j2}", "ws2_"],
                                      writes=[f"psz{half}"], tick=(hc == 7))
                            em.op("v", E_("tensor_add", gbuf[:], psz[half][:],
                                          ymt[j][:, dsl]),
                                  reads=[f"psz{half}", f"ymt{j}"],
                                  writes=["gbuf"])
                            em.op("s", E_("activation", gbuf[:], gbuf[:],
                                          AF.Silu), reads=["gbuf"],
                                  writes=["gbuf"])
                            em.op("v", E_("tensor_mul", gbuf[:], gbuf[:],
                                          sb_t[j][:, dsl]),
                                  reads=["gbuf", f"s{j}"], writes=["gbuf"])
                            em.op("v", E_("tensor_add", ob[j][:, dsl],
                                          xb2[j][:, dsl], gbuf[:]),
                                  reads=["gbuf", f"x2_{j}"],
                                  writes=[f"ob{j}"])
                        em.op("y", E_("dma_start", out=out[r0:r1, :],
                                      in_=ob[j][:]),
                              reads=[f"ob{j}"], dma=True)
                        if sb + 1 < SB:
                            chid_unit(sb + 1, 2 * tt + 1)
                if DEBUG:
                    em.op("y", E_("dma_start", out=dbg["d_s"], in_=s_tm),
                          dma=True)
                    em.op("y", E_("dma_start", out=dbg["d_xn"], in_=xn_tm),
                          dma=True)
                    em.op("y", E_("dma_start", out=dbg["d_gidx"], in_=gidx),
                          dma=True)
                    for z0 in range(0, NP + 128, 8192):
                        zn = min(8192, NP + 128 - z0)
                        em.op("y", E_("dma_start",
                                      out=dbg["d_pair"][z0:z0 + zn, :],
                                      in_=pairout[z0:z0 + zn, :]), dma=True)
                    em.op("y", E_("dma_start", out=dbg["d_w"], in_=w_all[:]),
                          reads=["w_all"], dma=True)
                    em.op("y", E_("dma_start", out=dbg["d_pos"],
                                  in_=pos_all[:]),
                          reads=["pos_all"], dma=True)
                    em.op("y", E_("dma_start", out=dbg["d_hfm"],
                                  in_=hfm_d), dma=True)
                    em.op("y", E_("dma_start", out=dbg["d_ot"], in_=ot),
                          dma=True)
                for key in sorted(em.dmakeys):
                    if em.cnt[key] > 0:
                        em.op("s", E_("wait_ge", em.sems[key], em.cnt[key]),
                              tick=False)

                for qi in range(len(em.qs)):
                    with nc.Block(f"blk{qi}") as blk:
                        blk.vector(lambda v, qi=qi: em.run_queue(qi, "v", v))
                        blk.scalar(lambda s, qi=qi: em.run_queue(qi, "s", s))
                        blk.tensor(lambda t, qi=qi: em.run_queue(qi, "t", t))
                        blk.gpsimd(lambda g, qi=qi: em.run_queue(qi, "g", g))
                        blk.sync(lambda y, qi=qi: em.run_queue(qi, "y", y))

    return nc


def kernel(**inputs):
    x_full = np.ascontiguousarray(np.asarray(inputs["L0_attn_x"], np.float32))
    B = x_full.shape[0]
    pe = positional_encoding(T, D)
    ident = np.eye(128, dtype=np.float32)
    identbf = np.eye(128).astype(ml_dtypes.bfloat16)
    iota32 = np.broadcast_to(np.arange(E, dtype=np.float32), (128, E)).copy()
    onescol2 = np.zeros((128, 2), ml_dtypes.bfloat16)
    onescol2[0:64, 0] = 1
    onescol2[64:128, 1] = 1
    sel2 = np.zeros((2, 256), np.float32)
    sel2[0, 0:128] = 1
    sel2[1, 128:256] = 1
    u128 = np.triu(np.ones((128, 128)), 1).astype(ml_dtypes.bfloat16)
    allones = np.ones((128, 128), ml_dtypes.bfloat16)
    tok4 = np.repeat((np.arange(NT, dtype=np.uint32)[None, :] * 128
                      + np.arange(128, dtype=np.uint32)[:, None]), 4, axis=1)
    tok4 = np.ascontiguousarray(tok4.astype(np.uint32))
    dumpcol4 = np.broadcast_to(
        (float(NP) + np.arange(128, dtype=np.float32)).reshape(128, 1),
        (128, 4)).copy()

    f32 = lambda k: np.ascontiguousarray(np.asarray(inputs[k], np.float32))
    row = lambda k: f32(k).reshape(1, -1)
    base = {
        "pe_tm": pe, "ident": ident, "identbf": identbf, "iota32": iota32,
        "onescol2": onescol2, "sel2": sel2, "u128": u128,
        "allones": allones, "tok4": tok4, "dumpcol4": dumpcol4,
        "Wq": f32("L0_attn_Wq"), "bq": row("L0_attn_bq"),
        "Wk": f32("L0_attn_Wk"), "bk": row("L0_attn_bk"),
        "Wv": f32("L0_attn_Wv"), "bv": row("L0_attn_bv"),
        "Wo": f32("L0_attn_Wo"), "bo": row("L0_attn_bo"),
        "gamma": row("L0_attn_gamma"), "beta": row("L0_attn_beta"),
        "Wg": f32("L0_ffn_Wg"),
        "W1": f32("L0_ffn_W1"), "W2": f32("L0_ffn_W2"), "W3": f32("L0_ffn_W3"),
        "Ws1": f32("L0_ffn_Ws1"), "Ws2": f32("L0_ffn_Ws2"),
        "Ws3": f32("L0_ffn_Ws3"),
    }
    nc = build_program()
    in_maps = [dict(base, x=np.ascontiguousarray(x_full[b])) for b in range(B)]
    res = run_bass_kernel_spmd(nc, in_maps, core_ids=list(range(B)))
    return np.stack([np.asarray(r["out"], np.float32) for r in res.results], 0)
